# version 02
# baseline (speedup 1.0000x reference)
import numpy as np
import ml_dtypes
import concourse.bass as bass
import concourse.mybir as mybir
from concourse.bass_utils import run_bass_kernel_spmd
from concourse.bass_core_rust import IndirectOffsetOnAxis

F32 = mybir.dt.float32
BF16 = mybir.dt.bfloat16
I32 = mybir.dt.int32
AF = mybir.ActivationFunctionType
ALU = mybir.AluOpType
AX = mybir.AxisListType

D = 1024
DC = D // 128
L3F_HALVES = 2
DEBUG_DUMPS = []
NCORES = 8
EPS_LN = 1e-5
BF = ml_dtypes.bfloat16

ENGINES = ("tensor", "vector", "scalar", "gpsimd", "sync")



def pack_lhsT_out_in(w):
    out, inn = w.shape
    oc, kc = out // 128, inn // 128
    return np.ascontiguousarray(w.reshape(oc, 128, kc, 128).transpose(0, 2, 3, 1))


def pack_lhsT_in_out(w):
    inn, out = w.shape
    kc, oc = inn // 128, out // 128
    return np.ascontiguousarray(w.reshape(kc, 128, oc, 128).transpose(2, 0, 1, 3))


def pack_scale(s):
    return np.ascontiguousarray(s.reshape(-1, 128).T)



SEM_NAMES = (["ph", "xl", "xw", "mm", "ev", "sc", "gs", "sy", "cc", "ts",
              "xd0", "xd1", "xs0", "xs1"]
             + [f"wr{i}" for i in range(8)]
             + [f"xc{i}" for i in range(8)]
             + [f"ax{i}" for i in range(10)])


class Flow:
    def __init__(self, nc):
        self.nc = nc
        self.sems = {}
        self.cnt = {n: 0 for n in SEM_NAMES}
        self.hw = {}
        self.queues = {e: [] for e in ENGINES}
        self.last_item = {e: None for e in ENGINES}
        self.fence_snap = {n: 0 for n in SEM_NAMES}

    def bump(self, sem, amt):
        self.cnt[sem] += amt
        return (sem, self.cnt[sem])

    def wait(self, eng, evt):
        if evt is None:
            return
        sem, val = evt
        key = (eng, sem)
        if self.hw.get(key, 0) >= val:
            return
        self.hw[key] = val
        self.queues[eng].append({"fn": None, "wait": (sem, val), "sem": None, "amt": 0})

    def emit(self, eng, fn, sem=None, amt=1, waits=()):
        merged = {}
        for w in waits:
            if w is None:
                continue
            sname, val = w
            if merged.get(sname, 0) < val:
                merged[sname] = val
        for sname, val in merged.items():
            self.wait(eng, (sname, val))
        item = {"fn": fn, "wait": None, "sem": sem, "amt": amt}
        evt = self.bump(sem, amt) if sem is not None else None
        self.queues[eng].append(item)
        self.last_item[eng] = item
        return evt

    def dma(self, eng, out, in_, sem="gs", waits=(), **kw):
        waits = [w for w in waits if w is not None]
        if sem == "gs":
            waits = list(waits) + [("gs", self.cnt["gs"])]
        return self.emit(eng, lambda e: e.dma_start(out=out, in_=in_, **kw),
                         sem=sem, amt=16, waits=waits)

    def fence(self):
        evts = []
        for e in ENGINES:
            it = self.last_item[e]
            if it is None:
                continue
            if it["sem"] is None:
                it["sem"] = "ph"
                it["amt"] = 1
                evts.append(self.bump("ph", 1))
        for n in SEM_NAMES:
            if self.cnt[n] > self.fence_snap[n]:
                evts.append((n, self.cnt[n]))
                self.fence_snap[n] = self.cnt[n]
        for e in ENGINES:
            for evt in evts:
                self.wait(e, evt)
            self.last_item[e] = None

    def replay(self, block):
        sems = self.sems

        def runner(engname):
            def _run(eng):
                for item in self.queues[engname]:
                    if item["wait"] is not None:
                        s, v = item["wait"]
                        eng.wait_ge(sems[s], v)
                    elif item["fn"] is not None:
                        inst = item["fn"](eng)
                        if item["sem"] is not None:
                            inst.then_inc(sems[item["sem"]], item["amt"])
            return _run

        block.tensor(runner("tensor"))
        block.vector(runner("vector"))
        block.scalar(runner("scalar"))
        block.gpsimd(runner("gpsimd"))
        block.sync(runner("sync"))


class Ctx:
    def __init__(self, nc, fl, T):
        self.nc = nc
        self.fl = fl
        self.T = T
        self.ap = {}
        self.sb = {}
        self.slab = None
        self.ps = []

    def reg(self, off, cols, part=128):
        return self.slab[0:part, off:off + cols]

    def regf(self, off, n_f32, pbase=0, parts=128):
        return self.slab[pbase:pbase + parts, off:off + 2 * n_f32].bitcast(F32)


SLAB_COLS = 101376

XB_O = 0
S1_O = 16384
S2_O = 49152
WR_O = 81920



def proj_stage(cx, wtiles, n_oc, n_kc, n_n, moving_fn, evac_fn,
               ring_slots, ring_cols, first_waits=(), bank_base=0, n_banks=4,
               ld_sem_base=0, ld_first_waits=()):
    fl = cx.fl
    WR = cx.reg(WR_O, 12288)
    ps = cx.ps
    slot_cols = n_kc * 128
    assert ring_slots * slot_cols <= ring_cols
    assert n_banks >= 2 * n_n
    ev_out = [None] * n_oc
    mm_evts = [None] * n_oc
    for oc in range(n_oc):
        slot = oc % ring_slots
        base = slot * slot_cols
        ld_waits = [mm_evts[oc - ring_slots]] if oc >= ring_slots else list(ld_first_waits)
        lw = fl.dma("gpsimd",
                    WR[:, base:base + slot_cols].rearrange("p (k m) -> p k m", m=128),
                    wtiles(oc).rearrange("k p m -> p k m"),
                    "wr%d" % ((ld_sem_base + slot) % 8), waits=ld_waits)
        grp = (oc % 2) * n_n
        banks = [ps[bank_base + grp + n] for n in range(n_n)]
        mm_waits = [lw]
        if oc == 0:
            mm_waits += list(first_waits)
        if oc >= 2:
            mm_waits.append(ev_out[oc - 2])
        last = None
        for n in range(n_n):
            for kc in range(n_kc):
                mv = moving_fn(kc, n)
                last = fl.emit(
                    "tensor",
                    lambda e, oc=oc, kc=kc, n=n, base=base, banks=banks, mv=mv: e.matmul(
                        banks[n][:, 0:512],
                        WR[:, base + kc * 128:base + (kc + 1) * 128],
                        mv,
                        start=(kc == 0), stop=(kc == n_kc - 1)),
                    sem=("mm" if (kc == n_kc - 1 and n == n_n - 1) else None),
                    waits=(mm_waits if (kc == 0 and n == 0) else ()))
        mm_evts[oc] = ("mm", fl.cnt["mm"])
        ev_out[oc] = evac_fn(oc, mm_evts[oc], banks)
    return ev_out



def emit_init(cx):
    fl = cx.fl
    e1 = fl.emit("vector", lambda e: e.memset(cx.sb["ONEC"][:, :], 1.0), sem="ev")
    e2 = fl.emit("vector", lambda e: e.memset(cx.sb["ONER"][:, :], 1.0), sem="ev")
    e3 = fl.emit("vector", lambda e: e.memset(cx.sb["SCL2"][:, 1499:1500], 1.0), sem="ev")
    e4 = fl.emit("vector", lambda e: e.memset(cx.sb["SCL2"][0:1, 1532:1660], 1.0), sem="ev")
    for eng in ("tensor", "scalar"):
        for ev_ in (e1, e2, e3, e4):
            fl.wait(eng, ev_)


def emit_ln(cx, src_ap, gain_view=None, bias_view=None, extra_wait=None,
            ab_cols=None, shift_half=False, raw_out=None):
    fl, T = cx.fl, cx.T
    nt = T // 512
    XB = cx.reg(XB_O, DC * T)
    SQ = cx.reg(S1_O, 2 * T)
    AB = cx.reg(S1_O + 2 * T, 2 * T)
    rf_o = S1_O + 4 * T
    rb_o = S1_O + 10 * T
    m_row = cx.regf(rf_o, T, 0, 1)
    ve_row = cx.regf(rf_o + 2 * T, T, 0, 1)
    sc_row = cx.regf(rf_o + 4 * T, T, 0, 1)
    a_row = cx.slab[0:1, rb_o:rb_o + T]
    b_row = cx.slab[0:1, rb_o + T:rb_o + 2 * T]
    onec, oner = cx.sb["ONEC"], cx.sb["ONER"]
    ps = cx.ps

    ld = []
    for c in range(DC):
        w0 = [extra_wait] if extra_wait else []
        if shift_half and c >= 4:
            ze = fl.emit("vector", lambda e, c=c: e.memset(XB[:, c * T:c * T + 1], 0.0),
                         sem="ev", waits=w0)
            ld.append(fl.dma("gpsimd", XB[:, c * T + 1:(c + 1) * T],
                             src_ap[c * 128:(c + 1) * 128, 0:T - 1], "xc%d" % c,
                             waits=w0 + [ze]))
        else:
            ld.append(fl.dma("gpsimd", XB[:, c * T:(c + 1) * T],
                             src_ap[c * 128:(c + 1) * 128, :], "xc%d" % c,
                             waits=w0))

    sq_ev = [None] * DC
    s2mm = [None] * DC
    for c in range(DC):
        half = c % 2
        sq_ev[c] = fl.emit(
            "scalar",
            lambda e, c=c, half=half: e.activation(
                SQ[:, half * T:(half + 1) * T], XB[:, c * T:(c + 1) * T], AF.Square),
            sem="sc",
            waits=[ld[c]] + ([s2mm[c - 2]] if c >= 2 else []))
        for n in range(nt):
            fl.emit("tensor",
                    lambda e, c=c, n=n: e.matmul(
                        ps[n][0:1, 0:512], onec[:, :],
                        XB[:, c * T + n * 512:c * T + (n + 1) * 512],
                        start=(c == 0), stop=(c == DC - 1)),
                    waits=([ld[c]] if n == 0 else ()))
        for n in range(nt):
            fl.emit("tensor",
                    lambda e, c=c, n=n, half=half: e.matmul(
                        ps[4 + n][0:1, 0:512], onec[:, :],
                        SQ[:, half * T + n * 512:half * T + (n + 1) * 512],
                        start=(c == 0), stop=(c == DC - 1)),
                    sem=("mm" if n == nt - 1 else None),
                    waits=([sq_ev[c]] if n == 0 else ()))
        s2mm[c] = ("mm", fl.cnt["mm"])

    stats_done = s2mm[DC - 1]

    inv = 1.0 / D
    ev = stats_done
    for n in range(nt):
        ev = fl.emit("vector", lambda e, n=n: e.tensor_scalar(
            m_row[0:1, n * 512:(n + 1) * 512], ps[n][0:1, 0:512], inv, None, ALU.mult),
            sem="ev", waits=[ev])
    for n in range(nt):
        ev = fl.emit("vector", lambda e, n=n: e.tensor_scalar(
            ve_row[0:1, n * 512:(n + 1) * 512], ps[4 + n][0:1, 0:512], inv, EPS_LN,
            ALU.mult, ALU.add), sem="ev", waits=[ev])
    ev = fl.emit("vector", lambda e: e.tensor_mul(sc_row, m_row, m_row), sem="ev", waits=[ev])
    ev = fl.emit("vector", lambda e: e.tensor_sub(ve_row, ve_row, sc_row), sem="ev", waits=[ev])
    ev = fl.emit("vector", lambda e: e.reciprocal(sc_row, ve_row), sem="ev", waits=[ev])
    ev = fl.emit("scalar", lambda e: e.activation(ve_row, sc_row, AF.Sqrt),
                 sem="sc", waits=[ev])
    ev = fl.emit("vector", lambda e: e.tensor_copy(a_row, ve_row), sem="ev", waits=[ev])
    ev = fl.emit("vector", lambda e: e.scalar_tensor_tensor(
        sc_row, m_row, -1.0, ve_row, ALU.mult, ALU.mult),
        sem="ev", waits=[ev])
    ev = fl.emit("vector", lambda e: e.tensor_copy(b_row, sc_row), sem="ev", waits=[ev])
    if ab_cols is not None:
        ACOL, BCOL = ab_cols
        onecf = cx.sb["SCL2"][0:1, 1499:1500]
        ntt_ = T // 128
        cpv = None
        for tt in range(ntt_):
            for j, (row, COL) in enumerate(((ve_row, ACOL), (sc_row, BCOL))):
                fl.emit("tensor", lambda e, tt=tt, row=row, j=j: e.matmul(
                    ps[j][:, 0:1], row[0:1, tt * 128:(tt + 1) * 128],
                    onecf, start=True, stop=True),
                    sem="mm", waits=[ev] + ([cpv] if cpv else []))
                tme = ("mm", fl.cnt["mm"])
                cpv = fl.emit("vector", lambda e, tt=tt, COL=COL, j=j: e.tensor_copy(
                    COL[:, tt:tt + 1], ps[j][:, 0:1]),
                    sem="ev", waits=[tme])
        ev = cpv

    ab_mm = []
    rows_ab = [a_row, b_row]
    for j in range(2):
        for n in range(nt):
            fl.emit("tensor", lambda e, j=j, n=n: e.matmul(
                ps[j * 4 + n][:, 0:512], oner[:, :],
                rows_ab[j][0:1, n * 512:(n + 1) * 512],
                start=True, stop=True),
                sem=("mm" if n == nt - 1 else None),
                waits=[ev])
        ab_mm.append(("mm", fl.cnt["mm"]))
    ce = ev
    for j in range(2):
        for n in range(nt):
            ce = fl.emit("vector", lambda e, j=j, n=n: e.tensor_copy(
                AB[:, j * T + n * 512:j * T + (n + 1) * 512], ps[j * 4 + n][:, 0:512]),
                sem="ev", waits=[ab_mm[j], ce])
    ab_done = ce

    ev = ab_done
    for c in range(DC):
        half = c % 2
        ev = fl.emit("vector", lambda e, c=c, half=half: e.tensor_mul(
            SQ[:, half * T:(half + 1) * T], XB[:, c * T:(c + 1) * T], AB[:, 0:T]),
            sem="ev", waits=[ev, sq_ev[DC - 1]])
        ev = fl.emit("vector", lambda e, c=c, half=half: e.tensor_add(
            XB[:, c * T:(c + 1) * T], SQ[:, half * T:(half + 1) * T], AB[:, T:2 * T]),
            sem="ev", waits=[ev])
        if gain_view is not None:
            ev = fl.emit("vector", lambda e, c=c: e.tensor_scalar(
                XB[:, c * T:(c + 1) * T], XB[:, c * T:(c + 1) * T],
                gain_view[:, c:c + 1], bias_view[:, c:c + 1], ALU.mult, ALU.add),
                sem="ev", waits=[ev])
    return ev


def load_scales(cx, specs):
    fl = cx.fl
    out = {}
    last = None
    for ap_name, cols, off in specs:
        view = cx.sb["SCL"][:, off:off + cols]
        last = fl.dma("gpsimd", view, cx.ap[ap_name], "gs")
        out[ap_name] = view
    return {k: (v, last) for k, v in out.items()}



def emit_l0ffn(cx, pfx, alpha):
    fl, T = cx.fl, cx.T
    th = T // 2
    nth = th // 512
    ps = cx.ps
    XB = cx.reg(XB_O, DC * T)
    S1 = cx.reg(S1_O, 32 * th)
    S2 = cx.reg(S2_O, 32 * th)
    UT = cx.reg(S2_O, 2048)
    XRT = cx.regf(S2_O, 2 * th)
    XOT = cx.regf(S2_O + 4 * th, 2 * th)
    scl = load_scales(cx, [(pfx + "s1", 32, 0), (pfx + "s2", 32, 32),
                           (pfx + "h1s", 32, 64), (pfx + "h2s", 32, 96),
                           (pfx + "s3", 8, 128)])
    s1v, s1e = scl[pfx + "s1"]
    s2v, s2e = scl[pfx + "s2"]
    h1sv, h1se = scl[pfx + "h1s"]
    h2sv, h2se = scl[pfx + "h2s"]
    s3v, s3e = scl[pfx + "s3"]
    s3hv = cx.sb["SCL"][:, 136:144]
    s3he = fl.emit("vector", lambda e: e.tensor_scalar(
        s3hv, s3v, float(alpha), None, ALU.mult), sem="ev", waits=[s3e])

    w1p, w2p = cx.ap[pfx + "w1p"], cx.ap[pfx + "w2p"]
    h1p, h2p = cx.ap[pfx + "h1p"], cx.ap[pfx + "h2p"]
    w3p = cx.ap[pfx + "w3p"]
    xres = cx.ap["xres"]

    ln_done = emit_ln(cx, xres)

    prev_half_done = None
    for hv in range(L3F_HALVES):
        tok0 = hv * th

        a_ev = _emit_stage_a(cx, w1p, w2p, s1v, s2v, XB, S1, UT, tok0, th,
                             first_waits=[ln_done] + ([prev_half_done] if prev_half_done else []),
                             s_evts=(s1e, s2e))

        b_ev = proj_stage(
            cx, lambda oc: h1p[oc], 32, 32, nth,
            moving_fn=lambda kc, n: S1[:, kc * th + n * 512:kc * th + (n + 1) * 512],
            evac_fn=_gelu_evac(cx, S2, th, h1sv, h1se, extra=[a_ev[-1]]),
            ring_slots=3, ring_cols=12288,
            first_waits=[a_ev[-1]], bank_base=4, ld_first_waits=[a_ev[-1]])
        c_ev = proj_stage(
            cx, lambda oc: h2p[oc], 32, 32, nth,
            moving_fn=lambda kc, n: S2[:, kc * th + n * 512:kc * th + (n + 1) * 512],
            evac_fn=_gelu_evac(cx, S1, th, h2sv, h2se, extra=[b_ev[-1]]),
            ring_slots=3, ring_cols=12288,
            first_waits=[b_ev[-1]], bank_base=4, ld_first_waits=[b_ev[-1]])
        d_ev = _emit_stage_d(cx, w3p, s3hv, s3he, S1, xres, tok0, th,
                             first_waits=[c_ev[-1]], XRT=XRT, XOT=XOT)
        prev_half_done = d_ev[-1]
        fl.fence()


def _emit_stage_a(cx, w1p, w2p, s1v, s2v, XB, S1, UT, tok0, th, first_waits, s_evts):
    fl, T = cx.fl, cx.T
    nth = th // 512
    WR = cx.reg(WR_O, 12288)
    ps = cx.ps
    s1e, s2e = s_evts
    sil_ev = [None] * 32
    stt_ev = [None] * 32
    mm_evts = [None] * 32
    for oc in range(32):
        slot = oc % 4
        base = slot * 2048
        ld_waits = [mm_evts[oc - 4]] if oc >= 4 else ()
        l1 = fl.dma("gpsimd", WR[:, base:base + 1024].rearrange("p (k m) -> p k m", m=128),
                    w1p[oc].rearrange("k p m -> p k m"), "wr%d" % slot, waits=ld_waits)
        l2 = fl.dma("gpsimd", WR[:, base + 1024:base + 2048].rearrange("p (k m) -> p k m", m=128),
                    w2p[oc].rearrange("k p m -> p k m"), "wr%d" % slot)
        grp = (oc % 2) * 4
        mm_waits = [l1, l2]
        if oc == 0:
            mm_waits += list(first_waits)
        if oc >= 2:
            mm_waits += [sil_ev[oc - 2], stt_ev[oc - 2]]
        last = None
        for wsel in range(2):
            for n in range(nth):
                for kc in range(DC):
                    last = fl.emit("tensor", lambda e, wsel=wsel, kc=kc, n=n, base=base, grp=grp: e.matmul(
                        ps[grp + wsel * 2 + n][:, 0:512],
                        WR[:, base + wsel * 1024 + kc * 128:base + wsel * 1024 + (kc + 1) * 128],
                        XB[:, kc * T + tok0 + n * 512:kc * T + tok0 + (n + 1) * 512],
                        start=(kc == 0), stop=(kc == DC - 1)),
                        sem=("mm" if (wsel == 1 and kc == DC - 1 and n == nth - 1) else None),
                        waits=(mm_waits if (wsel == 0 and kc == 0 and n == 0) else ()))
        mm_evts[oc] = ("mm", fl.cnt["mm"])
        se = None
        for n in range(nth):
            se = fl.emit("scalar", lambda e, oc=oc, n=n, grp=grp: e.activation(
                UT[:, (oc % 2) * 1024 + n * 512:(oc % 2) * 1024 + (n + 1) * 512],
                ps[grp + n][:, 0:512], AF.Silu, scale=s1v[:, oc:oc + 1]),
                sem="sc", waits=[mm_evts[oc], s1e]
                + ([sil_ev[oc - 1]] if oc >= 1 else [])
                + ([stt_ev[oc - 2]] if oc >= 2 else []))
        sil_ev[oc] = se
        de = None
        for n in range(nth):
            de = fl.emit("vector", lambda e, oc=oc, n=n, grp=grp: e.scalar_tensor_tensor(
                S1[:, oc * th + n * 512:oc * th + (n + 1) * 512],
                ps[grp + 2 + n][:, 0:512], s2v[:, oc:oc + 1],
                UT[:, (oc % 2) * 1024 + n * 512:(oc % 2) * 1024 + (n + 1) * 512],
                ALU.mult, ALU.mult),
                sem="ev", waits=[mm_evts[oc], se, s2e]
                + ([stt_ev[oc - 1]] if oc >= 1 else []))
        stt_ev[oc] = de
    return stt_ev


def _gelu_evac(cx, DST, th, sv, sev, extra=()):
    fl = cx.fl
    nth = th // 512
    state = {"prev": None}

    def evac(oc, mm_evt, banks):
        ge = None
        waits = [mm_evt, sev] + list(extra if oc == 0 else ()) \
            + ([state["prev"]] if state["prev"] else [])
        for n in range(nth):
            ge = fl.emit("scalar", lambda e, oc=oc, n=n, banks=banks: e.activation(
                DST[:, oc * th + n * 512:oc * th + (n + 1) * 512],
                banks[n][:, 0:512], AF.Gelu, scale=sv[:, oc:oc + 1]),
                sem="sc", waits=waits if n == 0 else [])
        state["prev"] = ge
        return ge

    return evac


def _emit_stage_d(cx, w3p, s3hv, s3he, SRC, xres, tok0, th, first_waits, XRT, XOT,
                  n_kc=32, moving_fn=None, wr_base_off=0, wr_region=None,
                  bias_mode=False, ring_slots=3):
    fl, T = cx.fl, cx.T
    nth = th // 512
    WR = wr_region if wr_region is not None else cx.reg(WR_O, 12288)
    ps = cx.ps
    if moving_fn is None:
        moving_fn = lambda kc, n: SRC[:, kc * th + n * 512:kc * th + (n + 1) * 512]
    slot_cols = n_kc * 128
    dn_ev = [None] * DC
    de_ev = [None] * DC
    mm_evts = [None] * DC
    for oc in range(DC):
        slot = oc % ring_slots
        base = wr_base_off + slot * slot_cols
        ld_waits = [mm_evts[oc - ring_slots]] if oc >= ring_slots else list(first_waits)
        lw = fl.dma("gpsimd", WR[:, base:base + slot_cols].rearrange("p (k m) -> p k m", m=128),
                    w3p[oc].rearrange("k p m -> p k m"), "wr%d" % slot, waits=ld_waits)
        xslot = (oc % 2) * th
        lx_waits = [de_ev[oc - 2]] if oc >= 2 else list(first_waits)
        lx = None
        for n in range(nth):
            lx = fl.dma("sync", XRT[:, xslot + n * 512:xslot + (n + 1) * 512],
                        xres[oc * 128:(oc + 1) * 128, tok0 + n * 512:tok0 + (n + 1) * 512],
                        "xd%d" % (oc % 2), waits=(lx_waits if n == 0 else ()))
        grp = (oc % 2) * 2
        mm_waits = [lw]
        if oc == 0:
            mm_waits += list(first_waits)
        if oc >= 2:
            mm_waits.append(de_ev[oc - 2])
        last = None
        for n in range(nth):
            for kc in range(n_kc):
                mv = moving_fn(kc, n)
                last = fl.emit("tensor", lambda e, kc=kc, n=n, base=base, grp=grp, mv=mv: e.matmul(
                    ps[grp + n][:, 0:512],
                    WR[:, base + kc * 128:base + (kc + 1) * 128],
                    mv,
                    start=(kc == 0), stop=(kc == n_kc - 1)),
                    sem=("mm" if (kc == n_kc - 1 and n == nth - 1) else None),
                    waits=(mm_waits if (kc == 0 and n == 0) else ()))
        mm_evts[oc] = ("mm", fl.cnt["mm"])
        de = None
        for n in range(nth):
            sc_arg = s3hv[:, oc:oc + 1] if s3hv is not None else 1.0
            op0 = ALU.add if bias_mode else ALU.mult
            de = fl.emit("vector", lambda e, oc=oc, n=n, grp=grp, xslot=xslot, sc_arg=sc_arg, op0=op0: e.scalar_tensor_tensor(
                XOT[:, xslot + n * 512:xslot + (n + 1) * 512],
                ps[grp + n][:, 0:512], sc_arg,
                XRT[:, xslot + n * 512:xslot + (n + 1) * 512],
                op0, ALU.add),
                sem="ev", waits=[mm_evts[oc], s3he, lx]
                + ([de_ev[oc - 1]] if oc >= 1 else [])
                + ([dn_ev[oc - 2]] if oc >= 2 else list(first_waits)))
        de_ev[oc] = de
        st = None
        for n in range(nth):
            st = fl.dma("sync", xres[oc * 128:(oc + 1) * 128, tok0 + n * 512:tok0 + (n + 1) * 512],
                        XOT[:, xslot + n * 512:xslot + (n + 1) * 512],
                        "xs%d" % (oc % 2), waits=([de] if n == 0 else ()))
        dn_ev[oc] = st
    return dn_ev



def _mk_sbuf(es, nc, name, shape, dt):
    return es.enter_context(nc.sbuf_tensor(name, shape, dt))


def build(T, emit_body, input_specs):
    import contextlib
    nc = bass.Bass("TRN2", target_bir_lowering=False)
    fl = Flow(nc)
    cx = Ctx(nc, fl, T)

    xT = nc.dram_tensor("xT", [D, T], F32, kind="ExternalInput").ap()
    yout = nc.dram_tensor("yout", [D, T], F32, kind="ExternalOutput").ap()
    cx.ap["xres"] = nc.dram_tensor("xres", [D, T], F32, kind="Internal").ap()
    cx.ap["zres"] = nc.dram_tensor("zres", [D, T], F32, kind="Internal").ap()
    cx.ap["xnspill"] = nc.dram_tensor("xnspill", [T, D], BF16, kind="Internal").ap()
    cx.ap["ybuf"] = nc.dram_tensor("ybuf", [T, D], F32, kind="Internal").ap()
    cx.ap["idxbuf"] = nc.dram_tensor("idxbuf", [2 * T], I32, kind="Internal").ap()
    cx.ap["gatebuf"] = nc.dram_tensor("gatebuf", [2 * T], F32, kind="Internal").ap()
    cx.ap["cin_kv"] = nc.dram_tensor("cin_kv", [2 * T, D], BF16, kind="Internal").ap()
    cx.ap["cout_kv"] = nc.dram_tensor("cout_kv", [16 * T, D], BF16,
                                      kind="Internal").ap()
    for name, (shape, dt) in input_specs.items():
        cx.ap[name] = nc.dram_tensor(name, list(shape), dt, kind="ExternalInput").ap()

    with contextlib.ExitStack() as es:
        for s in SEM_NAMES:
            fl.sems[s] = es.enter_context(nc.semaphore(s))
        cx.slab = _mk_sbuf(es, nc, "SLAB", [128, SLAB_COLS], BF16)
        cx.sb["SCL"] = _mk_sbuf(es, nc, "SCL", [128, 160], F32)
        cx.sb["SCL2"] = _mk_sbuf(es, nc, "SCL2", [128, 1700], F32)
        cx.sb["ONEC"] = _mk_sbuf(es, nc, "ONEC", [128, 1], BF16)
        cx.sb["ONER"] = _mk_sbuf(es, nc, "ONER", [1, 128], BF16)
        cx.sb["IDENT"] = _mk_sbuf(es, nc, "IDENT", [128, 128], BF16)
        cx.sb["IDF"] = _mk_sbuf(es, nc, "IDF", [128, 8], F32)
        for i in range(8):
            cx.ps.append(es.enter_context(nc.psum_tensor(f"PSB{i}", [128, 512], F32)))

        cpev = fl.dma("sync", cx.ap["xres"], xT, "sy")
        if "ident" in cx.ap:
            idl = fl.dma("gpsimd", cx.sb["IDENT"][:, :], cx.ap["ident"], "gs")
            idl2 = fl.dma("gpsimd", cx.sb["IDF"][:, :], cx.ap["ident"][:, 0:8], "gs")
            fl.wait("tensor", idl)
            fl.wait("tensor", idl2)
        emit_init(cx)
        for _e in ENGINES:
            fl.wait(_e, cpev)
        emit_body(cx)
        fl.fence()
        for name, off, cols, part in DEBUG_DUMPS:
            dap = nc.dram_tensor("dbg_" + name, [part, cols], BF16,
                                 kind="ExternalOutput").ap()
            fl.dma("sync", dap, cx.slab[0:part, off:off + cols], "sy")
        fl.fence()
        fev = fl.dma("sync", yout, cx.ap["xres"], "sy")
        fl.wait("sync", fev)

        with nc.Block() as block:
            fl.replay(block)

    return nc



def emit_l0attn(cx, pfx):
    fl, T = cx.fl, cx.T
    nt = T // 512
    ps = cx.ps
    XB = cx.reg(XB_O, DC * T)
    QT = cx.reg(16384, DC * T)
    KT = cx.reg(32768, DC * T)
    VT = cx.reg(49152, 16 * 1024)
    AO = cx.reg(65536, 16 * 1024)
    WR = cx.reg(WR_O, 12288)
    ntt = T // 128
    xres = cx.ap["xres"]
    ident = cx.sb["IDENT"]

    ln_done = emit_ln(cx, xres)

    def qk_evac(DST):
        state = {"prev": None}

        def evac(oc, mm_evt, banks):
            ce = None
            for n in range(nt):
                ce = fl.emit("vector", lambda e, oc=oc, n=n, banks=banks: e.tensor_copy(
                    DST[:, oc * T + n * 512:oc * T + (n + 1) * 512], banks[n][:, 0:512]),
                    sem="ev",
                    waits=([mm_evt] + ([state["prev"]] if state["prev"] else [])) if n == 0 else ())
            state["prev"] = ce
            return ce
        return evac

    q_ev = proj_stage(cx, lambda oc: cx.ap[pfx + "wqp"][oc], DC, DC, nt,
                      moving_fn=lambda kc, n: XB[:, kc * T + n * 512:kc * T + (n + 1) * 512],
                      evac_fn=qk_evac(QT), ring_slots=3, ring_cols=12288,
                      first_waits=[ln_done], bank_base=0, n_banks=8,
                      ld_first_waits=())
    k_ev = proj_stage(cx, lambda oc: cx.ap[pfx + "wkp"][oc], DC, DC, nt,
                      moving_fn=lambda kc, n: XB[:, kc * T + n * 512:kc * T + (n + 1) * 512],
                      evac_fn=qk_evac(KT), ring_slots=3, ring_cols=12288,
                      first_waits=[q_ev[-1]], bank_base=0, n_banks=8,
                      ld_first_waits=[q_ev[-1]])

    wvl = fl.dma("gpsimd", WR[:, 0:8192].rearrange("p (k n) -> p k n", n=1024),
                 cx.ap[pfx + "wvT"].rearrange("k p n -> p k n"), "wr3",
                 waits=[k_ev[-1]])
    v_mm = [None] * ntt
    v_ev = [None] * ntt
    for tt in range(ntt):
        grp = (tt % 2) * 2
        mm_waits = [wvl, k_ev[-1]] if tt == 0 else ([v_ev[tt - 2]] if tt >= 2 else [])
        last = None
        for n in range(2):
            for kc in range(DC):
                last = fl.emit("tensor", lambda e, tt=tt, n=n, kc=kc, grp=grp: e.matmul(
                    ps[grp + n][:, 0:512],
                    XB[:, kc * T + tt * 128:kc * T + tt * 128 + 128],
                    WR[:, kc * 1024 + n * 512:kc * 1024 + (n + 1) * 512],
                    start=(kc == 0), stop=(kc == DC - 1)),
                    sem=("mm" if (n == 1 and kc == DC - 1) else None),
                    waits=(mm_waits if (n == 0 and kc == 0) else ()))
        v_mm[tt] = ("mm", fl.cnt["mm"])
        ce = None
        for n in range(2):
            ce = fl.emit("vector", lambda e, tt=tt, n=n, grp=grp: e.tensor_copy(
                VT[:, tt * 1024 + n * 512:tt * 1024 + (n + 1) * 512], ps[grp + n][:, 0:512]),
                sem="ev", waits=([v_mm[tt]] + ([v_ev[tt - 1]] if tt >= 1 else [])) if n == 0 else ())
        v_ev[tt] = ce

    tbl = fl.dma("gpsimd", XB[:, 0:8 * T].rearrange("p (j n) -> p j n", n=T),
                 cx.ap[pfx + "ropeT"].rearrange("j p n -> p j n"), "gs",
                 waits=[v_mm[-1]])
    cosT = lambda c: XB[:, c * T:(c + 1) * T]
    sinT = lambda c: XB[:, (4 + c) * T:(5 + c) * T]
    T2a = AO[:, 0:T]
    T2b = AO[:, T:2 * T]
    TMPo = AO[:, 2 * T:3 * T]
    ev = tbl
    for SL, done in ((QT, q_ev[-1]), (KT, k_ev[-1])):
        for c in range(4):
            qc = SL[:, c * T:(c + 1) * T]
            qc4 = SL[:, (c + 4) * T:(c + 5) * T]
            ev = fl.emit("vector", lambda e, qc=qc, c=c: e.tensor_mul(T2a, qc, sinT(c)),
                         sem="ev", waits=[ev, tbl, done, v_ev[-1]])
            ev = fl.emit("vector", lambda e, qc4=qc4, c=c: e.tensor_mul(T2b, qc4, cosT(c)),
                         sem="ev", waits=[ev])
            ev = fl.emit("vector", lambda e: e.tensor_add(TMPo, T2a, T2b), sem="ev", waits=[ev])
            ev = fl.emit("vector", lambda e, qc=qc, c=c: e.tensor_mul(T2a, qc, cosT(c)),
                         sem="ev", waits=[ev])
            ev = fl.emit("vector", lambda e, qc4=qc4, c=c: e.tensor_mul(T2b, qc4, sinT(c)),
                         sem="ev", waits=[ev])
            ev = fl.emit("vector", lambda e, qc=qc: e.tensor_sub(qc, T2a, T2b), sem="ev", waits=[ev])
            ev = fl.emit("vector", lambda e, qc4=qc4: e.tensor_copy(qc4, TMPo), sem="ev", waits=[ev])
    rope_done = ev

    PT = WR[:, 0:2048]
    PR = WR[:, 2048:4096]
    NM4 = cx.sb["SCL"][:, 144:148]
    NMS = cx.sb["SCL"][:, 148:149]
    SM4 = cx.sb["SCL"][:, 149:153]
    SMT = cx.sb["SCL"][:, 153:154]
    RINV = cx.sb["SCL"][:, 154:155]
    inv_s = 1.0 / 32.0
    prev_exp = None
    prev_pv = None
    prev_ao = None
    ao_ev = [None] * ntt
    for tt in range(ntt):
        mm_waits = [rope_done] + ([prev_exp] if prev_exp else [])
        last = None
        for n in range(nt):
            for kc in range(DC):
                last = fl.emit("tensor", lambda e, tt=tt, n=n, kc=kc: e.matmul(
                    ps[n][:, 0:512],
                    QT[:, kc * T + tt * 128:kc * T + tt * 128 + 128],
                    KT[:, kc * T + n * 512:kc * T + (n + 1) * 512],
                    start=(kc == 0), stop=(kc == DC - 1)),
                    sem=("mm" if (n == nt - 1 and kc == DC - 1) else None),
                    waits=(mm_waits if (n == 0 and kc == 0) else ()))
        sc_mm = ("mm", fl.cnt["mm"])
        ev = None
        for n in range(nt):
            ev = fl.emit("vector", lambda e, n=n: e.reduce_max(
                NM4[:, n:n + 1], ps[n][:, 0:512], AX.X, negate=True),
                sem="ev", waits=[sc_mm] + ([prev_ao] if (n == 0 and prev_ao) else []))
        ev = fl.emit("vector", lambda e: e.tensor_reduce(
            NMS, NM4[:, 0:nt], AX.X, ALU.min), sem="ev", waits=[ev])
        nms_ev = fl.emit("vector", lambda e: e.tensor_scalar(
            NMS, NMS, inv_s, None, ALU.mult), sem="ev", waits=[ev])
        ee = None
        for n in range(nt):
            ee = fl.emit("scalar", lambda e, n=n: e.activation(
                PR[:, n * 512:(n + 1) * 512], ps[n][:, 0:512], AF.Exp,
                bias=NMS, scale=inv_s, accum_out=SM4[:, n:n + 1]),
                sem="sc", waits=[nms_ev] + ([ee] if ee else []))
        exp_ev = ee
        prev_exp = exp_ev
        ev = fl.emit("vector", lambda e: e.reduce_sum(SMT, SM4[:, 0:nt], AX.X), sem="ev", waits=[exp_ev])
        rinv_ev = fl.emit("vector", lambda e: e.reciprocal(RINV, SMT), sem="ev", waits=[ev])
        pb = [cx.ps[4][:, 0:512].bitcast(BF16), cx.ps[5][:, 0:512].bitcast(BF16)]
        tr_mm = [None] * ntt
        cp_ev = [None] * ntt
        for t in range(ntt):
            sl = pb[t % 2][:, 0:128]
            trw = [exp_ev]
            if t >= 2:
                trw.append(cp_ev[t - 2])
            if tt > 0 and t < 2:
                trw.append(prev_pv)
            tr_mm[t] = fl.emit("tensor", lambda e, t=t, sl=sl: e.transpose(
                sl, PR[:, t * 128:(t + 1) * 128], ident[:, :]),
                sem="mm", waits=trw)
            cp_ev[t] = fl.emit("vector", lambda e, t=t, sl=sl: e.tensor_copy(
                PT[:, t * 128:(t + 1) * 128], sl),
                sem="ev", waits=[tr_mm[t]] + ([prev_pv] if tt > 0 else []))
        mm_waits = [cp_ev[ntt - 1]] + ([prev_ao] if prev_ao else [])
        for n in range(2):
            for kc in range(ntt):
                last = fl.emit("tensor", lambda e, n=n, kc=kc: e.matmul(
                    ps[6 + n][:, 0:512],
                    PT[:, kc * 128:(kc + 1) * 128],
                    VT[:, kc * 1024 + n * 512:kc * 1024 + (n + 1) * 512],
                    start=(kc == 0), stop=(kc == ntt - 1)),
                    sem=("mm" if (n == 1 and kc == ntt - 1) else None),
                    waits=(mm_waits if (n == 0 and kc == 0) else ()))
        pv_mm = ("mm", fl.cnt["mm"])
        prev_pv = pv_mm
        ae = None
        for n in range(2):
            ae = fl.emit("vector", lambda e, tt=tt, n=n: e.tensor_scalar(
                AO[:, tt * 1024 + n * 512:tt * 1024 + (n + 1) * 512],
                ps[6 + n][:, 0:512], RINV, None, ALU.mult),
                sem="ev", waits=[pv_mm, rinv_ev] + ([prev_ao] if prev_ao else []))
        ao_ev[tt] = ae
        prev_ao = ae

    AOT = XB
    pb = [cx.ps[4][:, 0:512].bitcast(BF16), cx.ps[5][:, 0:512].bitcast(BF16)]
    tr_mm = [None] * (ntt * DC)
    cp_ev = [None] * (ntt * DC)
    i = 0
    for tt in range(ntt):
        for c in range(DC):
            sl = pb[i % 2][:, 0:128]
            trw = [ao_ev[tt]]
            if i >= 2:
                trw.append(cp_ev[i - 2])
            if i == 0:
                trw.append(prev_pv)
            tr_mm[i] = fl.emit("tensor", lambda e, tt=tt, c=c, sl=sl: e.transpose(
                sl, AO[:, tt * 1024 + c * 128:tt * 1024 + (c + 1) * 128], ident[:, :]),
                sem="mm", waits=trw)
            cp_ev[i] = fl.emit("vector", lambda e, tt=tt, c=c, sl=sl: e.tensor_copy(
                AOT[:, c * T + tt * 128:c * T + tt * 128 + 128], sl),
                sem="ev", waits=[tr_mm[i], ao_ev[-1] if i == 0 else None])
            i += 1
    aot_done = cp_ev[-1]

    XRT = cx.regf(16384, T)
    XOT = cx.regf(16384 + 2 * T, T)
    fw = [aot_done]
    for hv in range(2):
        th = T // 2
        tok0 = hv * th
        d_ev = _emit_stage_d(cx, cx.ap[pfx + "wop"], None, None, None, xres, tok0, th,
                             first_waits=fw, XRT=XRT, XOT=XOT,
                             n_kc=DC,
                             moving_fn=lambda kc, n, tok0=tok0: AOT[:, kc * T + tok0 + n * 512:kc * T + tok0 + (n + 1) * 512],
                             wr_base_off=4096)
        fw = [d_ev[-1], d_ev[-2], ("ev", cx.fl.cnt["ev"])]
    fl.fence()



def emit_l1attn(cx, pfx):
    fl, T = cx.fl, cx.T
    nt = T // 512
    ps = cx.ps
    XB = cx.reg(XB_O, DC * T)
    QT = cx.reg(16384, DC * T)
    KT = cx.reg(32768, DC * T)
    VT = cx.reg(49152, 16 * 1024)
    AO = cx.reg(65536, 16 * 1024)
    WR = cx.reg(WR_O, 12288)
    ntt = T // 128
    xres = cx.ap["xres"]
    ident = cx.sb["IDENT"]

    ln_done = emit_ln(cx, xres)

    def qk_evac(DST):
        state = {"prev": None}

        def evac(oc, mm_evt, banks):
            ce = None
            for n in range(nt):
                ce = fl.emit("vector", lambda e, oc=oc, n=n, banks=banks: e.tensor_copy(
                    DST[:, oc * T + n * 512:oc * T + (n + 1) * 512], banks[n][:, 0:512]),
                    sem="ev",
                    waits=([mm_evt] + ([state["prev"]] if state["prev"] else [])) if n == 0 else ())
            state["prev"] = ce
            return ce
        return evac

    q_ev = proj_stage(cx, lambda oc: cx.ap[pfx + "wqp"][oc], DC, DC, nt,
                      moving_fn=lambda kc, n: XB[:, kc * T + n * 512:kc * T + (n + 1) * 512],
                      evac_fn=qk_evac(QT), ring_slots=3, ring_cols=12288,
                      first_waits=[ln_done], bank_base=0, n_banks=8)
    k_ev = proj_stage(cx, lambda oc: cx.ap[pfx + "wkp"][oc], DC, DC, nt,
                      moving_fn=lambda kc, n: XB[:, kc * T + n * 512:kc * T + (n + 1) * 512],
                      evac_fn=qk_evac(KT), ring_slots=3, ring_cols=12288,
                      first_waits=[q_ev[-1]], bank_base=0, n_banks=8,
                      ld_first_waits=[q_ev[-1]])
    wvl = fl.dma("gpsimd", WR[:, 0:8192].rearrange("p (k n) -> p k n", n=1024),
                 cx.ap[pfx + "wvT"].rearrange("k p n -> p k n"), "wr3",
                 waits=[k_ev[-1]])
    v_mm = [None] * ntt
    v_ev = [None] * ntt
    for tt in range(ntt):
        grp = (tt % 2) * 2
        mm_waits = [wvl, k_ev[-1]] if tt == 0 else ([v_ev[tt - 2]] if tt >= 2 else [])
        last = None
        for n in range(2):
            for kc in range(DC):
                last = fl.emit("tensor", lambda e, tt=tt, n=n, kc=kc, grp=grp: e.matmul(
                    ps[grp + n][:, 0:512],
                    XB[:, kc * T + tt * 128:kc * T + tt * 128 + 128],
                    WR[:, kc * 1024 + n * 512:kc * 1024 + (n + 1) * 512],
                    start=(kc == 0), stop=(kc == DC - 1)),
                    sem=("mm" if (n == 1 and kc == DC - 1) else None),
                    waits=(mm_waits if (n == 0 and kc == 0) else ()))
        v_mm[tt] = ("mm", fl.cnt["mm"])
        ce = None
        for n in range(2):
            ce = fl.emit("vector", lambda e, tt=tt, n=n, grp=grp: e.tensor_copy(
                VT[:, tt * 1024 + n * 512:tt * 1024 + (n + 1) * 512], ps[grp + n][:, 0:512]),
                sem="ev", waits=([v_mm[tt]] + ([v_ev[tt - 1]] if tt >= 1 else [])) if n == 0 else ())
        v_ev[tt] = ce

    CM = cx.regf(WR_O + 4096, 4 * 512)
    cml = fl.dma("gpsimd", CM.rearrange("p (o n) -> p o n", n=512),
                 cx.ap[pfx + "cmask"].rearrange("o p n -> p o n"),
                 "gs", waits=[v_mm[-1]])

    PT = WR[:, 0:2048]
    PR = WR[:, 2048:4096]
    NM4 = cx.sb["SCL"][:, 144:148]
    NMS = cx.sb["SCL"][:, 148:149]
    SM4 = cx.sb["SCL"][:, 149:153]
    SMT = cx.sb["SCL"][:, 153:154]
    RINV = cx.sb["SCL"][:, 154:155]
    inv_s = 1.0 / 32.0
    prev_exp = None
    prev_pv = None
    prev_ao = None
    prev_msk = None
    ao_ev = [None] * ntt
    for tt in range(ntt):
        nlive = tt // 4 + 1
        o = tt % 4
        mm_waits = [v_ev[-1], cml] + ([prev_exp] if prev_exp else [])
        last = None
        for n in range(nlive):
            for kc in range(DC):
                last = fl.emit("tensor", lambda e, tt=tt, n=n, kc=kc: e.matmul(
                    ps[n][:, 0:512],
                    QT[:, kc * T + tt * 128:kc * T + tt * 128 + 128],
                    KT[:, kc * T + n * 512:kc * T + (n + 1) * 512],
                    start=(kc == 0), stop=(kc == DC - 1)),
                    sem=("mm" if (n == nlive - 1 and kc == DC - 1) else None),
                    waits=(mm_waits if (n == 0 and kc == 0) else ()))
        sc_mm = ("mm", fl.cnt["mm"])
        mev = fl.emit("vector", lambda e, nd=nlive - 1, o=o: e.tensor_add(
            ps[nd][:, 0:512], ps[nd][:, 0:512], CM[:, o * 512:(o + 1) * 512]),
            sem="ev", waits=[sc_mm, cml])
        ev = mev
        for n in range(nlive):
            ev = fl.emit("vector", lambda e, n=n: e.reduce_max(
                NM4[:, n:n + 1], ps[n][:, 0:512], AX.X, negate=True),
                sem="ev", waits=[ev] + ([prev_ao] if (n == 0 and prev_ao) else []))
        ev = fl.emit("vector", lambda e, nlive=nlive: e.tensor_reduce(
            NMS, NM4[:, 0:nlive], AX.X, ALU.min), sem="ev", waits=[ev])
        nms_ev = fl.emit("vector", lambda e: e.tensor_scalar(
            NMS, NMS, inv_s, None, ALU.mult), sem="ev", waits=[ev])
        ee = None
        for n in range(nlive):
            ee = fl.emit("scalar", lambda e, n=n: e.activation(
                PR[:, n * 512:(n + 1) * 512], ps[n][:, 0:512], AF.Exp,
                bias=NMS, scale=inv_s, accum_out=SM4[:, n:n + 1]),
                sem="sc", waits=[nms_ev] + ([ee] if ee else []))
        exp_ev = ee
        prev_exp = exp_ev
        ev = fl.emit("vector", lambda e, nlive=nlive: e.reduce_sum(
            SMT, SM4[:, 0:nlive], AX.X), sem="ev", waits=[exp_ev])
        rinv_ev = fl.emit("vector", lambda e: e.reciprocal(RINV, SMT), sem="ev", waits=[ev])
        ntk = tt + 1
        pb = [cx.ps[4][:, 0:512].bitcast(BF16), cx.ps[5][:, 0:512].bitcast(BF16)]
        tr_mm = [None] * ntk
        cp_ev = [None] * ntk
        for t in range(ntk):
            sl = pb[t % 2][:, 0:128]
            trw = [exp_ev]
            if t >= 2:
                trw.append(cp_ev[t - 2])
            if tt > 0 and t < 2:
                trw.append(prev_pv)
            tr_mm[t] = fl.emit("tensor", lambda e, t=t, sl=sl: e.transpose(
                sl, PR[:, t * 128:(t + 1) * 128], ident[:, :]),
                sem="mm", waits=trw)
            cp_ev[t] = fl.emit("vector", lambda e, t=t, sl=sl: e.tensor_copy(
                PT[:, t * 128:(t + 1) * 128], sl),
                sem="ev", waits=[tr_mm[t]] + ([prev_pv] if tt > 0 else []))
        mm_waits = [cp_ev[ntk - 1]] + ([prev_ao] if prev_ao else [])
        for n in range(2):
            for kc in range(ntk):
                last = fl.emit("tensor", lambda e, n=n, kc=kc, ntk=ntk: e.matmul(
                    ps[6 + n][:, 0:512],
                    PT[:, kc * 128:(kc + 1) * 128],
                    VT[:, kc * 1024 + n * 512:kc * 1024 + (n + 1) * 512],
                    start=(kc == 0), stop=(kc == ntk - 1)),
                    sem=("mm" if (n == 1 and kc == ntk - 1) else None),
                    waits=(mm_waits if (n == 0 and kc == 0) else ()))
        pv_mm = ("mm", fl.cnt["mm"])
        prev_pv = pv_mm
        ae = None
        for n in range(2):
            ae = fl.emit("vector", lambda e, tt=tt, n=n: e.tensor_scalar(
                AO[:, tt * 1024 + n * 512:tt * 1024 + (n + 1) * 512],
                ps[6 + n][:, 0:512], RINV, None, ALU.mult),
                sem="ev", waits=[pv_mm, rinv_ev] + ([prev_ao] if prev_ao else []))
        ao_ev[tt] = ae
        prev_ao = ae

    AOT = XB
    pb = [cx.ps[4][:, 0:512].bitcast(BF16), cx.ps[5][:, 0:512].bitcast(BF16)]
    trm = [None] * (ntt * DC)
    cpv = [None] * (ntt * DC)
    i = 0
    for tt in range(ntt):
        for c in range(DC):
            sl = pb[i % 2][:, 0:128]
            trw = [ao_ev[tt]]
            if i >= 2:
                trw.append(cpv[i - 2])
            if i == 0:
                trw.append(prev_pv)
            trm[i] = fl.emit("tensor", lambda e, tt=tt, c=c, sl=sl: e.transpose(
                sl, AO[:, tt * 1024 + c * 128:tt * 1024 + (c + 1) * 128], ident[:, :]),
                sem="mm", waits=trw)
            cpv[i] = fl.emit("vector", lambda e, tt=tt, c=c, sl=sl: e.tensor_copy(
                AOT[:, c * T + tt * 128:c * T + tt * 128 + 128], sl),
                sem="ev", waits=[trm[i], ao_ev[-1] if i == 0 else None])
            i += 1
    aot_done = cpv[-1]

    XRTa = [cx.regf(16384, T), cx.regf(16384 + 2 * T, T)]
    XOTa = [cx.regf(16384 + 4 * T, T), cx.regf(16384 + 6 * T, T)]
    de_ev = [None] * DC
    st_ev = [None] * DC
    for c in range(DC):
        par = c % 2
        lxw = [de_ev[c - 2]] if c >= 2 else [aot_done]
        lx = fl.dma("sync", XRTa[par], xres[c * 128:(c + 1) * 128, :],
                    "xd%d" % par, waits=lxw)
        de = fl.emit("vector", lambda e, c=c, par=par: e.scalar_tensor_tensor(
            XOTa[par], AOT[:, c * T:(c + 1) * T], 1.0, XRTa[par],
            ALU.mult, ALU.add),
            sem="ev", waits=[lx, aot_done]
            + ([de_ev[c - 1]] if c >= 1 else [])
            + ([st_ev[c - 2]] if c >= 2 else []))
        de_ev[c] = de
        st_ev[c] = fl.dma("sync", xres[c * 128:(c + 1) * 128, :], XOTa[par],
                          "xs%d" % par, waits=[de])
    fl.fence()



def emit_l3attn(cx, pfx):
    fl, T = cx.fl, cx.T
    nt = T // 512
    ps = cx.ps
    XB = cx.reg(XB_O, DC * T)
    QT = cx.reg(16384, DC * T)
    KT = cx.reg(32768, DC * T)
    V2 = cx.reg(49152, DC * T)
    VT = cx.reg(65536, 16 * 1024)
    AO2 = cx.reg(81920, DC * T)
    ntt = T // 128
    xres = cx.ap["xres"]
    ident = cx.sb["IDENT"]
    SCL2 = cx.sb["SCL2"]
    Rf32 = SCL2[0:1, 0:512]
    Rb = SCL2[0:1, 512:768].bitcast(BF16)
    BCS = SCL2[:, 768:1024].bitcast(BF16)
    BCS2 = SCL2[:, 1024:1280].bitcast(BF16)
    E2 = SCL2[:, 1280:1281].bitcast(BF16)
    B2 = SCL2[0:2, 1281:1345].bitcast(BF16)
    onec = cx.sb["ONEC"]

    ln_done = emit_ln(cx, xres)

    def qk_evac(DST):
        state = {"prev": None}

        def evac(oc, mm_evt, banks):
            ce = None
            for n in range(nt):
                ce = fl.emit("vector", lambda e, oc=oc, n=n, banks=banks: e.tensor_copy(
                    DST[:, oc * T + n * 512:oc * T + (n + 1) * 512], banks[n][:, 0:512]),
                    sem="ev",
                    waits=([mm_evt] + ([state["prev"]] if state["prev"] else [])) if n == 0 else ())
            state["prev"] = ce
            return ce
        return evac

    q_ev = proj_stage(cx, lambda oc: cx.ap[pfx + "wqp"][oc], DC, DC, nt,
                      moving_fn=lambda kc, n: XB[:, kc * T + n * 512:kc * T + (n + 1) * 512],
                      evac_fn=qk_evac(QT), ring_slots=3, ring_cols=12288,
                      first_waits=[ln_done], bank_base=0, n_banks=8)
    k_ev = proj_stage(cx, lambda oc: cx.ap[pfx + "wkp"][oc], DC, DC, nt,
                      moving_fn=lambda kc, n: XB[:, kc * T + n * 512:kc * T + (n + 1) * 512],
                      evac_fn=qk_evac(KT), ring_slots=3, ring_cols=12288,
                      first_waits=[q_ev[-1]], bank_base=0, n_banks=8,
                      ld_first_waits=[q_ev[-1]])
    v_ev = proj_stage(cx, lambda oc: cx.ap[pfx + "wvp"][oc], DC, DC, nt,
                      moving_fn=lambda kc, n: XB[:, kc * T + n * 512:kc * T + (n + 1) * 512],
                      evac_fn=qk_evac(V2), ring_slots=3, ring_cols=12288,
                      first_waits=[k_ev[-1]], bank_base=0, n_banks=8,
                      ld_first_waits=[k_ev[-1]])

    vt_cp = [None] * (ntt * DC)
    vt_mm = [None] * (ntt * DC)
    i = 0
    for tt in range(ntt):
        for c in range(DC):
            bank = ps[i % 2]
            trw = [v_ev[-1]] if i == 0 else ([vt_cp[i - 2]] if i >= 2 else [])
            vt_mm[i] = fl.emit("tensor", lambda e, tt=tt, c=c, bank=bank: e.matmul(
                bank[:, 0:128], V2[:, c * T + tt * 128:c * T + tt * 128 + 128],
                ident[:, :], start=True, stop=True),
                sem="mm", waits=trw)
            vt_cp[i] = fl.emit("vector", lambda e, tt=tt, c=c, bank=bank: e.tensor_copy(
                VT[:, tt * 1024 + c * 128:tt * 1024 + (c + 1) * 128], bank[:, 0:128]),
                sem="ev", waits=[vt_mm[i]])
            i += 1
    vt_done = vt_cp[-1]

    e2a = fl.emit("vector", lambda e: e.memset(E2, 0.0), sem="ev")
    e2b = fl.emit("vector", lambda e: e.memset(E2[0:64, 0:1], 1.0), sem="ev", waits=[e2a])
    e2c = fl.emit("vector", lambda e: e.memset(E2[64:128, 1:2], 1.0), sem="ev", waits=[e2b])
    b2mm = fl.emit("tensor", lambda e: e.matmul(
        ps[3][0:2, 0:128], E2, ident[:, :], start=True, stop=True),
        sem="mm", waits=[e2c, vt_done])
    eb_done = fl.emit("vector", lambda e: e.tensor_copy(B2, ps[3][0:2, 0:128]),
                      sem="ev", waits=[b2mm])

    sxsz = ntt * 512
    SXo = [0, sxsz]
    inv_s = 1.0 / 8.0
    prev_exp_bank = [None] * 8
    prev_pv_ev = None
    prev_sum_ev = None
    prev_bc_cp = None
    ao_done = []
    for h in range(16):
        ch, b = h // 2, 64 * (h % 2)
        for g in range(nt):
            buf = (h * nt + g) % 2
            SX = XB[:, SXo[buf]:SXo[buf] + sxsz]
            exp_evs = [None] * ntt
            sc_mms = [None] * ntt
            for kc in range(ntt):
                bank = ps[kc % 6]
                scw = [vt_done, eb_done, prev_exp_bank[kc % 6]]
                if h * nt + g >= 2:
                    scw.append(ao_done[-2])
                sc_mms[kc] = fl.emit("tensor", lambda e, kc=kc, bank=bank, ch=ch, b=b, g=g: e.matmul(
                    bank[:, 0:512],
                    KT[b:b + 64, ch * T + kc * 128:ch * T + kc * 128 + 128],
                    QT[b:b + 64, ch * T + g * 512:ch * T + (g + 1) * 512],
                    start=True, stop=True),
                    sem="mm", waits=scw)
                ee = fl.emit("scalar", lambda e, kc=kc, bank=bank, SX=SX: e.activation(
                    SX[:, kc * 512:(kc + 1) * 512], bank[:, 0:512], AF.Exp,
                    scale=inv_s),
                    sem="sc", waits=[sc_mms[kc]])
                exp_evs[kc] = ee
                prev_exp_bank[kc % 6] = ee
            pvw = [prev_pv_ev] if prev_pv_ev else []
            smw = [prev_sum_ev] if prev_sum_ev else []
            for kc in range(ntt):
                fl.emit("tensor", lambda e, kc=kc, b=b, h=h, SX=SX: e.matmul(
                    ps[6][b:b + 64, 0:512],
                    VT[:, kc * 1024 + h * 64:kc * 1024 + h * 64 + 64],
                    SX[:, kc * 512:(kc + 1) * 512],
                    start=(kc == 0), stop=(kc == ntt - 1),
                    tile_position=(0, b)),
                    sem=("mm" if kc == ntt - 1 else None),
                    waits=([exp_evs[kc]] + (pvw if kc == 0 else [])))
            pv_mm = ("mm", fl.cnt["mm"])
            for kc in range(ntt):
                fl.emit("tensor", lambda e, kc=kc, SX=SX: e.matmul(
                    ps[7][0:1, 0:512], onec[:, :],
                    SX[:, kc * 512:(kc + 1) * 512],
                    start=(kc == 0), stop=(kc == ntt - 1)),
                    sem=("mm" if kc == ntt - 1 else None),
                    waits=([exp_evs[kc]] + (smw if kc == 0 else [])))
            sum_mm = ("mm", fl.cnt["mm"])
            rcp = fl.emit("vector", lambda e: e.reciprocal(Rf32, ps[7][0:1, 0:512]),
                          sem="ev", waits=[sum_mm])
            rcp = fl.emit("vector", lambda e: e.tensor_copy(Rb, Rf32),
                          sem="ev", waits=[rcp])
            fl.emit("tensor", lambda e: e.matmul(
                ps[7][:, 0:512], cx.sb["ONER"][:, :], Rb,
                start=True, stop=True),
                sem="mm", waits=[rcp])
            bc_mm = ("mm", fl.cnt["mm"])
            BCSx = BCS if buf == 0 else BCS2
            bcp = fl.emit("vector", lambda e, BCSx=BCSx: e.tensor_copy(
                BCSx[:, 0:512], ps[7][:, 0:512]),
                sem="ev", waits=[bc_mm] + ([prev_bc_cp] if prev_bc_cp else []))
            prev_bc_cp = bcp
            prev_sum_ev = bcp
            ae = fl.emit("vector", lambda e, ch=ch, b=b, g=g, BCSx=BCSx: e.tensor_tensor(
                AO2[b:b + 64, ch * T + g * 512:ch * T + (g + 1) * 512],
                ps[6][b:b + 64, 0:512], BCSx[b:b + 64, 0:512], ALU.mult),
                sem="ev", waits=[pv_mm, bcp])
            prev_pv_ev = ae
            ao_done.append(ae)
    att_done = ao_done[-1]

    DOT = cx.reg(16384, DC * T, part=2)
    VNS = cx.reg(65536, DC * T, part=2)
    SCR = KT
    xsa_ap = cx.sb["SCL"][0:2, 155:156]
    xl = fl.dma("gpsimd", xsa_ap, cx.ap[pfx + "xsa"], "gs")
    prev = att_done
    prev_cp = None
    for phase, (SRC2, DSTR) in enumerate(((AO2, DOT), (None, VNS))):
        for c in range(DC):
            if phase == 0:
                pe = fl.emit("vector", lambda e, c=c: e.tensor_mul(
                    SCR[:, c * T:(c + 1) * T], AO2[:, c * T:(c + 1) * T],
                    V2[:, c * T:(c + 1) * T]),
                    sem="ev", waits=[prev])
            else:
                pe = fl.emit("scalar", lambda e, c=c: e.activation(
                    SCR[:, c * T:(c + 1) * T], V2[:, c * T:(c + 1) * T], AF.Square),
                    sem="sc", waits=[prev])
            mm_e = None
            for n in range(nt):
                fl.emit("tensor", lambda e, c=c, n=n: e.matmul(
                    ps[n % 2][0:2, 0:512], E2,
                    SCR[:, c * T + n * 512:c * T + (n + 1) * 512],
                    start=True, stop=True),
                    sem="mm", waits=[pe, eb_done] + ([prev_cp] if prev_cp else []))
                mm_n = ("mm", fl.cnt["mm"])
                prev_cp = fl.emit("vector", lambda e, c=c, n=n, DSTR=DSTR: e.tensor_copy(
                    DSTR[0:2, (c * nt + n) * 512:(c * nt + n + 1) * 512],
                    ps[n % 2][0:2, 0:512]),
                    sem="ev", waits=[mm_n])
            prev = prev_cp
    ev = fl.emit("vector", lambda e: e.tensor_scalar(
        VNS, VNS, 1e-8, None, ALU.add), sem="ev", waits=[prev, xl])
    def _rcp_lp(e):
        with cx.nc.allow_low_precision(reason="bf16 xsa factor row (small correction term)"):
            return e.reciprocal(VNS, VNS)
    ev = fl.emit("vector", _rcp_lp, sem="ev", waits=[ev])
    ev = fl.emit("vector", lambda e: e.tensor_mul(VNS, VNS, DOT), sem="ev", waits=[ev])
    f_done = fl.emit("vector", lambda e: e.tensor_scalar(
        VNS, VNS, xsa_ap[0:2, 0:1], None, ALU.mult), sem="ev", waits=[ev])
    prev_bc = None
    prev_tt = None
    for c in range(DC):
        for n in range(nt):
            i = c * nt + n
            fl.emit("tensor", lambda e, i=i: e.matmul(
                ps[i % 2][:, 0:512], B2, VNS[0:2, i * 512:(i + 1) * 512],
                start=True, stop=True),
                sem="mm", waits=[f_done] + ([prev_bc] if prev_bc else []))
            bmm = ("mm", fl.cnt["mm"])
            BCSx = BCS if i % 2 == 0 else BCS2
            bcp = fl.emit("vector", lambda e, BCSx=BCSx, i=i: e.tensor_copy(
                BCSx[:, 0:512], ps[i % 2][:, 0:512]),
                sem="ev", waits=[bmm] + ([prev_bc] if prev_bc else []))
            t1 = fl.emit("vector", lambda e, c=c, n=n, BCSx=BCSx: e.tensor_mul(
                SCR[:, c * T + n * 512:c * T + (n + 1) * 512],
                BCSx[:, 0:512], V2[:, c * T + n * 512:c * T + (n + 1) * 512]),
                sem="ev", waits=[bcp])
            t2 = fl.emit("vector", lambda e, c=c, n=n: e.tensor_sub(
                AO2[:, c * T + n * 512:c * T + (n + 1) * 512],
                AO2[:, c * T + n * 512:c * T + (n + 1) * 512],
                SCR[:, c * T + n * 512:c * T + (n + 1) * 512]),
                sem="ev", waits=[t1])
            prev_bc = t1
            prev_tt = t2
    xsa_done = prev_tt

    XRT = cx.regf(16384, T)
    XOT = cx.regf(16384 + 2 * T, T)
    fw = [xsa_done]
    for hv in range(2):
        th = T // 2
        tok0 = hv * th
        d_ev = _emit_stage_d(cx, cx.ap[pfx + "wop"], None, None, None, xres, tok0, th,
                             first_waits=fw, XRT=XRT, XOT=XOT,
                             n_kc=DC,
                             moving_fn=lambda kc, n, tok0=tok0: AO2[:, kc * T + tok0 + n * 512:kc * T + tok0 + (n + 1) * 512],
                             wr_region=cx.reg(32768, 3072))
        fw = [d_ev[-1], d_ev[-2]]
    fl.fence()



def emit_l3ffn(cx, pfx, alpha):
    fl, T = cx.fl, cx.T
    th = T // 2
    nth = th // 512
    nt = T // 512
    ntt = T // 128
    ps = cx.ps
    XB = cx.reg(XB_O, DC * T)
    HP = cx.reg(16384, 32 * th)
    SH = cx.reg(16384 + 32 * th, 8 * th)
    G_O = 16384 + 40 * th
    G = cx.reg(G_O, 4 * th)
    U = cx.reg(G_O + 4 * th, 4 * th)
    RWB_O = G_O + 8 * th
    RWE_O = RWB_O + 2 * th
    WRO2 = RWE_O + 2 * th
    WR = cx.reg(WRO2, 6144)
    xres = cx.ap["xres"]
    ident = cx.sb["IDENT"]
    onec, oner = cx.sb["ONEC"], cx.sb["ONER"]

    SH_O = 16384 + 32 * th
    LG = cx.regf(SH_O, T, 0, 8)
    XF = cx.regf(SH_O + 2 * T, T)
    RW = cx.reg(86016, ntt * 8)
    SCL2 = cx.sb["SCL2"]
    M8 = SCL2[:, 1400:1408]
    NM1 = SCL2[:, 1408:1409]
    S1 = SCL2[:, 1409:1410]
    R1 = SCL2[:, 1410:1411]
    LGT1 = SCL2[:, 1411:1419]
    W8 = SCL2[:, 1419:1427]
    W8M = SCL2[:, 1427:1435]
    WRF = SCL2[:, 1435:1499]
    onecf = SCL2[:, 1499:1500]
    ACOL = SCL2[:, 1500:1516]
    BCOL = SCL2[:, 1516:1532]
    ONERF = SCL2[0:1, 1532:1660]
    s_row = SCL2[0:1, 1660:1668]
    SEB = SCL2[:, 1668:1676]
    EBB = SCL2[:, 1676:1684]
    ebr = SCL2[0:1, 1684:1692]
    T1 = SCL2[:, 1692:1700]
    ebl = fl.dma("gpsimd", ebr, cx.ap[pfx + "ebr"], "gs")
    wrl = fl.dma("gpsimd", WRF.rearrange("p (k m) -> p k m", m=8),
                 cx.ap[pfx + "wrp"].rearrange("k p m -> p k m"), "wr0")

    ln_done = emit_ln(cx, xres, ab_cols=(ACOL, BCOL))

    xf_prev = None
    lg_mm = None
    for kc in range(DC):
        xfl = fl.dma("sync", XF, xres[kc * 128:(kc + 1) * 128, :], "xd0",
                     waits=[xf_prev] if xf_prev else [ln_done])
        last = None
        for n in range(nt):
            last = fl.emit("tensor", lambda e, kc=kc, n=n: e.matmul(
                ps[n % 4][0:8, 0:512], WRF[:, kc * 8:(kc + 1) * 8],
                XF[:, n * 512:(n + 1) * 512],
                start=(kc == 0), stop=(kc == DC - 1)),
                sem=("mm" if n == nt - 1 else None),
                waits=([xfl, wrl, ln_done] if (kc == 0 and n == 0) else [xfl]))
        xf_prev = ("mm", fl.cnt["mm"])
        fl.emit("tensor", lambda e, kc=kc: e.matmul(
            ps[4][0:1, 0:8], onecf, WRF[:, kc * 8:(kc + 1) * 8],
            start=(kc == 0), stop=(kc == DC - 1)),
            sem="mm", waits=())
        lg_mm = ("mm", fl.cnt["mm"])
    ce = None
    for n in range(nt):
        ce = fl.emit("vector", lambda e, n=n: e.tensor_copy(
            LG[0:8, n * 512:(n + 1) * 512], ps[n % 4][0:8, 0:512]),
            sem="ev", waits=[lg_mm] if n == 0 else [])
    ce = fl.emit("vector", lambda e: e.tensor_copy(s_row, ps[4][0:1, 0:8]),
                 sem="ev", waits=[ce])
    fl.emit("tensor", lambda e: e.matmul(
        ps[5][:, 0:8], ONERF, s_row, start=True, stop=True),
        sem="mm", waits=[ce])
    bse = ("mm", fl.cnt["mm"])
    fl.emit("tensor", lambda e: e.matmul(
        ps[6][:, 0:8], ONERF, ebr, start=True, stop=True),
        sem="mm", waits=[ce, ebl])
    beb = ("mm", fl.cnt["mm"])
    ce = fl.emit("vector", lambda e: e.tensor_copy(SEB, ps[5][:, 0:8]),
                 sem="ev", waits=[bse, ce])
    lg_done = fl.emit("vector", lambda e: e.tensor_copy(EBB, ps[6][:, 0:8]),
                      sem="ev", waits=[beb, ce])

    idf = cx.sb["IDF"]
    rw_tile_ev = [None] * ntt
    for t in range(ntt):
        bank = ps[4 + t % 2]
        fl.emit("tensor", lambda e, t=t, bank=bank: e.matmul(
            bank[0:128, 0:8], LG[0:8, t * 128:(t + 1) * 128],
            idf[0:8, 0:8], start=True, stop=True),
            sem="mm", waits=[lg_done] + ([rw_tile_ev[t - 2]] if t >= 2 else []))
        tmm = ("mm", fl.cnt["mm"])
        ev = fl.emit("vector", lambda e, t=t, bank=bank: e.tensor_copy(
            LGT1, bank[0:128, 0:8]),
            sem="ev", waits=[tmm] + ([rw_tile_ev[t - 1]] if t >= 1 else []))
        ev = fl.emit("vector", lambda e, t=t: e.tensor_scalar(
            LGT1, LGT1, ACOL[:, t:t + 1], None, ALU.mult), sem="ev", waits=[ev])
        ev = fl.emit("vector", lambda e, t=t: e.tensor_scalar(
            T1, SEB, BCOL[:, t:t + 1], None, ALU.mult), sem="ev", waits=[ev])
        ev = fl.emit("vector", lambda e: e.tensor_add(LGT1, LGT1, T1), sem="ev", waits=[ev])
        ev = fl.emit("vector", lambda e: e.tensor_add(LGT1, LGT1, EBB), sem="ev", waits=[ev])
        ev = fl.emit("vector", lambda e: e.max(out=M8, in_=LGT1), sem="ev", waits=[ev])
        ev = fl.emit("vector", lambda e: e.tensor_scalar(
            NM1, M8[:, 0:1], -1.0, None, ALU.mult), sem="ev", waits=[ev])
        ev = fl.emit("scalar", lambda e: e.activation(
            W8, LGT1, AF.Exp, bias=NM1),
            sem="sc", waits=[ev])
        ev = fl.emit("vector", lambda e: e.tensor_scalar(
            W8M, LGT1, M8[:, 1:2], None, ALU.is_ge), sem="ev", waits=[ev])
        ev = fl.emit("vector", lambda e: e.tensor_mul(W8M, W8M, W8), sem="ev", waits=[ev])
        ev = fl.emit("vector", lambda e: e.reduce_sum(S1, W8M, AX.X), sem="ev", waits=[ev])
        ev = fl.emit("vector", lambda e: e.reciprocal(R1, S1), sem="ev", waits=[ev])
        ev = fl.emit("vector", lambda e, t=t: e.tensor_scalar(
            RW[:, t * 8:(t + 1) * 8], W8M, R1, None, ALU.mult), sem="ev", waits=[ev])
        rw_tile_ev[t] = ev
    RWR = cx.reg(94208, T, part=8)
    rwr_ev = [None] * ntt
    for t in range(ntt):
        bank = ps[4 + t % 2]
        tmm = fl.emit("tensor", lambda e, t=t, bank=bank: e.matmul(
            bank[0:8, 0:128], RW[:, t * 8:(t + 1) * 8],
            ident[:, :], start=True, stop=True),
            sem="mm", waits=[rw_tile_ev[-1]] + ([rwr_ev[t - 2]] if t >= 2 else []))
        tmm = ("mm", fl.cnt["mm"])
        rwr_ev[t] = fl.emit("vector", lambda e, t=t, bank=bank: e.tensor_copy(
            RWR[0:8, t * 128:(t + 1) * 128], bank[0:8, 0:128]),
            sem="ev", waits=[tmm] + ([rwr_ev[t - 1]] if t >= 1 else []))
    rwr_done = rwr_ev[-1]

    wgp = cx.ap[pfx + "wgp"]
    wup = cx.ap[pfx + "wup"]
    sgp = cx.ap[pfx + "sgp"]
    sup = cx.ap[pfx + "sup"]
    wdall = cx.ap[pfx + "wdall"]

    XRT = cx.regf(G_O, 2 * th)
    XOT = cx.regf(G_O + 4 * th, 2 * th)
    RWBA_O = 86016
    RWBall = [cx.reg(RWBA_O + e_ * th, th) for e_ in range(8)]

    prev_half_done = None
    for hv in range(L3F_HALVES):
        tok0 = hv * th
        hp_ev = []
        gmms = []
        bc_prev = None
        rwe_prev = None
        for e_ in range(8):
            rw_w = [rwr_done] if e_ < 2 else [bc_prev]
            if hv == 1 and e_ < 2:
                rw_w.append(prev_half_done)
            if rwe_prev is not None:
                rw_w.append(rwe_prev)
            rwe_prev = fl.dma(
                "gpsimd", cx.slab[0:1, RWE_O + (e_ % 2) * th:RWE_O + (e_ % 2) * th + th],
                cx.slab[e_:e_ + 1, 94208 + tok0:94208 + tok0 + th],
                "wr%d" % (3 + e_ % 2), waits=rw_w)
            for n in range(nth):
                bank = ps[6 + (e_ % 2)]
                fl.emit("tensor", lambda e, e_=e_, n=n, bank=bank: e.matmul(
                    bank[:, 0:512], oner[:, :],
                    cx.slab[0:1, RWE_O + (e_ % 2) * th + n * 512:RWE_O + (e_ % 2) * th + (n + 1) * 512],
                    start=True, stop=True),
                    sem="mm", waits=[rwe_prev] + ([bc_prev] if bc_prev else []))
                bm = ("mm", fl.cnt["mm"])
                bc_prev = fl.emit("vector", lambda e, e_=e_, n=n, bank=bank: e.tensor_copy(
                    RWBall[e_][:, n * 512:(n + 1) * 512], bank[:, 0:512]),
                    sem="ev", waits=[bm])
        rwb_all_done = bc_prev
        for e_ in range(8):
            rwb_done = rwb_all_done
            for oc in range(4):
                u_ = e_ * 4 + oc
                slot = u_ % 3
                base = slot * 2048
                if len(gmms) >= 3:
                    lww = [gmms[-3]]
                elif hv == 0:
                    lww = [lg_mm]
                else:
                    lww = [prev_half_done]
                lw1 = fl.dma("gpsimd", WR[:, base:base + 1024].rearrange("p (k m) -> p k m", m=128),
                             wgp[u_].rearrange("k p m -> p k m"), "wr%d" % slot, waits=lww)
                lw2 = fl.dma("gpsimd", WR[:, base + 1024:base + 2048].rearrange("p (k m) -> p k m", m=128),
                             wup[u_].rearrange("k p m -> p k m"), "wr%d" % slot)
                grp = (u_ % 2) * 4
                mmw = [lw1, lw2, rwb_done]
                if len(hp_ev) >= 2:
                    mmw.append(hp_ev[-2])
                if hv == 0 and u_ == 0:
                    mmw.append(ln_done)
                last = None
                for w_, woff in ((0, 0), (1, 1024)):
                    for n in range(nth):
                        for kc in range(DC):
                            last = fl.emit("tensor", lambda e, w_=w_, n=n, kc=kc, base=base, woff=woff, grp=grp, tok0=tok0: e.matmul(
                                ps[grp + w_ * 2 + n][:, 0:512],
                                WR[:, base + woff + kc * 128:base + woff + (kc + 1) * 128],
                                XB[:, kc * T + tok0 + n * 512:kc * T + tok0 + (n + 1) * 512],
                                start=(kc == 0), stop=(kc == DC - 1)),
                                sem=("mm" if (w_ == 1 and n == nth - 1 and kc == DC - 1) else None),
                                waits=(mmw if (w_ == 0 and n == 0 and kc == 0) else ()))
                gu_mm = ("mm", fl.cnt["mm"])
                gmms.append(gu_mm)
                se = None
                for n in range(nth):
                    se = fl.emit("scalar", lambda e, oc=oc, n=n, grp=grp: e.activation(
                        G[:, oc * th + n * 512:oc * th + (n + 1) * 512],
                        ps[grp + n][:, 0:512], AF.Silu),
                        sem="sc", waits=([gu_mm] + ([hp_ev[-1]] if hp_ev else [])) if n == 0 else [])
                de = None
                for n in range(nth):
                    de = fl.emit("vector", lambda e, oc=oc, n=n, grp=grp: e.scalar_tensor_tensor(
                        U[:, oc * th + n * 512:oc * th + (n + 1) * 512],
                        ps[grp + 2 + n][:, 0:512], 1.0,
                        G[:, oc * th + n * 512:oc * th + (n + 1) * 512],
                        ALU.mult, ALU.mult),
                        sem="ev", waits=([gu_mm, se] + ([hp_ev[-1]] if hp_ev else [])) if n == 0 else [se])
                he = None
                for n in range(nth):
                    he = fl.emit("vector", lambda e, e_=e_, oc=oc, n=n: e.tensor_mul(
                        HP[:, (e_ * 4 + oc) * th + n * 512:(e_ * 4 + oc) * th + (n + 1) * 512],
                        U[:, oc * th + n * 512:oc * th + (n + 1) * 512],
                        RWBall[e_][:, n * 512:(n + 1) * 512]),
                        sem="ev", waits=[de])
                hp_ev.append(he)
        sh_ev = []
        for oc in range(DC):
            slot = oc % 3
            base = slot * 2048
            if oc < 3:
                last_u = [u for u in range(32) if u % 3 == slot][-1]
                lww = [gmms[last_u]]
            else:
                lww = [gmms[32 + oc - 3]]
            lw1 = fl.dma("gpsimd", WR[:, base:base + 1024].rearrange("p (k m) -> p k m", m=128),
                         sgp[oc].rearrange("k p m -> p k m"), "wr%d" % slot, waits=lww)
            lw2 = fl.dma("gpsimd", WR[:, base + 1024:base + 2048].rearrange("p (k m) -> p k m", m=128),
                         sup[oc].rearrange("k p m -> p k m"), "wr%d" % slot)
            grp = (oc % 2) * 4
            mmw = [lw1, lw2]
            if len(sh_ev) >= 2:
                mmw.append(sh_ev[-2])
            else:
                mmw.append(hp_ev[-2 + (oc % 2)])
            last = None
            for w_, woff in ((0, 0), (1, 1024)):
                for n in range(nth):
                    for kc in range(DC):
                        last = fl.emit("tensor", lambda e, w_=w_, n=n, kc=kc, base=base, woff=woff, grp=grp, tok0=tok0: e.matmul(
                            ps[grp + w_ * 2 + n][:, 0:512],
                            WR[:, base + woff + kc * 128:base + woff + (kc + 1) * 128],
                            XB[:, kc * T + tok0 + n * 512:kc * T + tok0 + (n + 1) * 512],
                            start=(kc == 0), stop=(kc == DC - 1)),
                            sem=("mm" if (w_ == 1 and n == nth - 1 and kc == DC - 1) else None),
                            waits=(mmw if (w_ == 0 and n == 0 and kc == 0) else ()))
            gu_mm = ("mm", fl.cnt["mm"])
            gmms.append(gu_mm)
            se = None
            for n in range(nth):
                se = fl.emit("scalar", lambda e, oc=oc, n=n, grp=grp: e.activation(
                    G[:, (oc % 4) * th + n * 512:(oc % 4) * th + (n + 1) * 512],
                    ps[grp + n][:, 0:512], AF.Silu),
                    sem="sc", waits=([gu_mm, hp_ev[-1]] + ([sh_ev[-1]] if sh_ev else [])) if n == 0 else [])
            de = None
            for n in range(nth):
                de = fl.emit("vector", lambda e, oc=oc, n=n, grp=grp: e.scalar_tensor_tensor(
                    SH[:, oc * th + n * 512:oc * th + (n + 1) * 512],
                    ps[grp + 2 + n][:, 0:512], 1.0,
                    G[:, (oc % 4) * th + n * 512:(oc % 4) * th + (n + 1) * 512],
                    ALU.mult, ALU.mult),
                    sem="ev", waits=([gu_mm, se] + ([hp_ev[-1]] if hp_ev else [])
                                     + ([sh_ev[-1]] if sh_ev else [])) if n == 0 else [se])
            sh_ev.append(de)
        dn_ev = [None] * DC
        de_ev2 = [None] * DC
        dmms = [None] * DC
        for oc in range(DC):
            slot = oc % 2
            base = 6144 + slot * 5120
            lww = [dmms[oc - 2]] if oc >= 2 else [sh_ev[-1]]
            lw = fl.dma("gpsimd",
                        cx.reg(WRO2 + base, 5120).rearrange("p (k m) -> p k m", m=128),
                        wdall[oc].rearrange("k p m -> p k m"), "wr%d" % (5 + slot if slot else 5),
                        waits=lww)
            lxw = [de_ev2[oc - 2]] if oc >= 2 else [sh_ev[-1], gmms[-1]]
            lx = None
            for n in range(nth):
                lx = fl.dma("sync", XRT[:, (oc % 2) * th + n * 512:(oc % 2) * th + (n + 1) * 512],
                            xres[oc * 128:(oc + 1) * 128, tok0 + n * 512:tok0 + (n + 1) * 512],
                            "xd%d" % (oc % 2), waits=(lxw if n == 0 else ()))
            grp = (oc % 2) * 2
            mmw = [lw, sh_ev[-1]]
            if oc >= 2:
                mmw.append(de_ev2[oc - 2])
            last = None
            for n in range(nth):
                for kc in range(40):
                    mv = (HP[:, kc * th + n * 512:kc * th + (n + 1) * 512] if kc < 32
                          else SH[:, (kc - 32) * th + n * 512:(kc - 32) * th + (n + 1) * 512])
                    last = fl.emit("tensor", lambda e, n=n, kc=kc, base=base, grp=grp, mv=mv: e.matmul(
                        ps[grp + n][:, 0:512],
                        cx.reg(WRO2 + base, 5120)[:, kc * 128:(kc + 1) * 128],
                        mv, start=(kc == 0), stop=(kc == 39)),
                        sem=("mm" if (n == nth - 1 and kc == 39) else None),
                        waits=(mmw if (n == 0 and kc == 0) else ()))
            dmms[oc] = ("mm", fl.cnt["mm"])
            de = None
            for n in range(nth):
                de = fl.emit("vector", lambda e, oc=oc, n=n, grp=grp, alpha=alpha: e.scalar_tensor_tensor(
                    XOT[:, (oc % 2) * th + n * 512:(oc % 2) * th + (n + 1) * 512],
                    ps[grp + n][:, 0:512], float(alpha),
                    XRT[:, (oc % 2) * th + n * 512:(oc % 2) * th + (n + 1) * 512],
                    ALU.mult, ALU.add),
                    sem="ev", waits=[dmms[oc], lx]
                    + ([de_ev2[oc - 1]] if oc >= 1 else [])
                    + ([dn_ev[oc - 2]] if oc >= 2 else []))
            de_ev2[oc] = de
            st = None
            for n in range(nth):
                st = fl.dma("sync", xres[oc * 128:(oc + 1) * 128, tok0 + n * 512:tok0 + (n + 1) * 512],
                            XOT[:, (oc % 2) * th + n * 512:(oc % 2) * th + (n + 1) * 512],
                            "xs%d" % (oc % 2), waits=([de] if n == 0 else ()))
            dn_ev[oc] = st
        prev_half_done = dn_ev[-1]
        fl.fence()



def emit_shift_load(cx, src_ap, extra_wait=None):
    fl, T = cx.fl, cx.T
    XB = cx.reg(XB_O, DC * T)
    ld = []
    for c in range(DC):
        w0 = [extra_wait] if extra_wait else []
        if c >= 4:
            ze = fl.emit("vector", lambda e, c=c: e.memset(XB[:, c * T:c * T + 1], 0.0),
                         sem="ev", waits=w0)
            ld.append(fl.dma("gpsimd", XB[:, c * T + 1:(c + 1) * T],
                             src_ap[c * 128:(c + 1) * 128, 0:T - 1], "xc%d" % c,
                             waits=w0 + [ze]))
        else:
            ld.append(fl.dma("gpsimd", XB[:, c * T:(c + 1) * T],
                             src_ap[c * 128:(c + 1) * 128, :], "xc%d" % c, waits=w0))
    return ld


L2_Q_O, L2_K_O, L2_V_O = 16384, 24576, 32768
L2_QT_O, L2_KT_O = 40960, 49152
L2_PR_O, L2_PT_O = 57344, 58368
L2_AO_O, L2_AOT_O = 59392, 67584
L2_LAM_O = 75776
L2_XPT_O = 83968
L2_QS_O = 86016
L2_WQKV_O = 87040
L2_A_O, L2_G_O = 32768, 55296
L2_WR2_O = 77824
L2_XRT_O = 83968


def emit_l2ffn(cx, pfx, li):
    fl, T = cx.fl, cx.T
    th = T // 2
    nth = th // 512
    nt = T // 512
    ntt = T // 128
    ps = cx.ps
    XB = cx.reg(XB_O, DC * T)
    xres = cx.ap["xres"]
    zres = cx.ap["zres"]
    ident = cx.sb["IDENT"]
    onec, oner = cx.sb["ONEC"], cx.sb["ONER"]
    SCL = cx.sb["SCL"]

    ln_done = emit_ln(cx, xres)
    zcp = []
    for c in range(DC):
        zcp.append(fl.dma("gpsimd", zres[c * 128:(c + 1) * 128, :],
                          XB[:, c * T:(c + 1) * T], "xc%d" % c, waits=[ln_done]))
    fl.fence()
    prev_done = zcp[-1]

    for l in range(2):
        prev_done = _emit_l2_layer(cx, pfx, l, prev_done)
        fl.fence()

    XRT = cx.regf(83968, T)
    XOT = cx.regf(88064, T)
    ZRT = cx.regf(92160, T)
    de_ev = [None] * DC
    st_ev = [None] * DC
    for c in range(DC):
        par = c % 2
        lww = [de_ev[c - 1]] if c >= 1 else [prev_done]
        lx = fl.dma("sync", XRT, xres[c * 128:(c + 1) * 128, :],
                    "xd%d" % par, waits=lww)
        lz = fl.dma("sync", ZRT, zres[c * 128:(c + 1) * 128, :], "xd%d" % par)
        de = fl.emit("vector", lambda e, c=c: e.scalar_tensor_tensor(
            XOT, ZRT, 0.5, XRT, ALU.mult, ALU.add),
            sem="ev", waits=[lx, lz]
            + ([de_ev[c - 1]] if c >= 1 else [])
            + ([st_ev[c - 1]] if c >= 1 else []))
        de_ev[c] = de
        st_ev[c] = fl.dma("sync", xres[c * 128:(c + 1) * 128, :], XOT,
                          "xs%d" % par, waits=[de])
    fl.fence()


def _emit_l2_layer(cx, pfx, l, prev_done):
    fl, T = cx.fl, cx.T
    th = T // 2
    nth = th // 512
    nt = T // 512
    ntt = T // 128
    ps = cx.ps
    XB = cx.reg(XB_O, DC * T)
    zres = cx.ap["zres"]
    ident = cx.sb["IDENT"]
    SCL = cx.sb["SCL"]
    Q = cx.reg(L2_Q_O, 16 * 512)
    Kt = cx.reg(L2_K_O, 16 * 512)
    V = cx.reg(L2_V_O, 16 * 512)
    QT = cx.reg(L2_QT_O, 4 * T)
    KT = cx.reg(L2_KT_O, 4 * T)
    PR = cx.reg(L2_PR_O, 1024)
    PT = cx.reg(L2_PT_O, 1024)
    AO = cx.reg(L2_AO_O, 16 * 512)
    AOT = cx.reg(L2_AOT_O, 4 * T)
    LAM = cx.reg(L2_LAM_O, 8 * 1024)
    WQ = cx.reg(L2_WQKV_O, 8 * 1536)
    QSB = cx.reg(L2_QS_O, 512)
    KSB = cx.reg(L2_QS_O + 512, 512)
    SCL2 = cx.sb["SCL2"]

    lam_l = fl.dma("gpsimd", LAM.rearrange("p (v n) -> p v n", n=1024),
                   cx.ap[pfx + "lam"].rearrange("v p n -> p v n"), "gs",
                   waits=[prev_done])
    xw = ntt * 32
    xpt_l = fl.dma("gpsimd", cx.reg(L2_XPT_O, 4 * xw).rearrange("p (j n) -> p j n", n=xw),
                   cx.ap[pfx + "xpt"].rearrange("j p n -> p j n"), "gs")
    XPT = [cx.reg(L2_XPT_O + j * xw, xw) for j in range(4)]
    qsl = fl.dma("gpsimd", cx.slab[0:1, L2_PR_O:L2_PR_O + 512],
                 cx.ap[pfx + "qsr"][l:l + 1, :], "gs")
    ksl = fl.dma("gpsimd", cx.slab[0:1, L2_PR_O + 512:L2_PR_O + 1024],
                 cx.ap[pfx + "ksr"][l:l + 1, :], "gs")
    bmm = fl.emit("tensor", lambda e: e.matmul(
        ps[0][:, 0:512], cx.sb["ONER"][:, :],
        cx.slab[0:1, L2_PR_O:L2_PR_O + 512], start=True, stop=True),
        sem="mm", waits=[qsl, prev_done])
    b1 = ("mm", fl.cnt["mm"])
    fl.emit("tensor", lambda e: e.matmul(
        ps[1][:, 0:512], cx.sb["ONER"][:, :],
        cx.slab[0:1, L2_PR_O + 512:L2_PR_O + 1024], start=True, stop=True),
        sem="mm", waits=[ksl])
    b2 = ("mm", fl.cnt["mm"])
    c1 = fl.emit("vector", lambda e: e.tensor_copy(QSB, ps[0][:, 0:512]),
                 sem="ev", waits=[b1])
    qs_done = fl.emit("vector", lambda e: e.tensor_copy(KSB, ps[1][:, 0:512]),
                      sem="ev", waits=[b2, c1])

    hb_ld = emit_shift_load(cx, zres, extra_wait=prev_done)
    wql = fl.dma("gpsimd", WQ.rearrange("p (k n) -> p k n", n=1536),
                 cx.ap[pfx + "wqkv"][l].rearrange("k p n -> p k n"), "wr0",
                 waits=[prev_done])
    qkv_mm = [None] * ntt
    qkv_ev = [None] * ntt
    for tt in range(ntt):
        grp = (tt % 2) * 3
        mmw = [wql, qs_done] + hb_ld if tt == 0 else ([qkv_ev[tt - 2]] if tt >= 2 else [])
        last = None
        for n in range(3):
            for kc in range(DC):
                last = fl.emit("tensor", lambda e, tt=tt, n=n, kc=kc, grp=grp: e.matmul(
                    ps[grp + n][:, 0:512],
                    XB[:, kc * T + tt * 128:kc * T + tt * 128 + 128],
                    WQ[:, kc * 1536 + n * 512:kc * 1536 + (n + 1) * 512],
                    start=(kc == 0), stop=(kc == DC - 1)),
                    sem=("mm" if (n == 2 and kc == DC - 1) else None),
                    waits=(mmw if (n == 0 and kc == 0) else ()))
        qkv_mm[tt] = ("mm", fl.cnt["mm"])
        ce = None
        for n, DST in ((0, Q), (1, Kt), (2, V)):
            ce = fl.emit("vector", lambda e, tt=tt, n=n, DST=DST, grp=grp: e.tensor_copy(
                DST[:, tt * 512:(tt + 1) * 512], ps[grp + n][:, 0:512]),
                sem="ev", waits=([qkv_mm[tt]] + ([qkv_ev[tt - 1]] if tt >= 1 else [])) if n == 0 else [])
        qkv_ev[tt] = ce

    SC1 = cx.reg(L2_AO_O, 512)
    RN8 = SCL2[:, 1400:1408]
    T128 = cx.reg(L2_AO_O + 512, 512)
    rope_ev = [None] * ntt
    pev = qkv_ev[-1]
    for tt in range(ntt):
        for SLB, SCB, CT, ST in ((Q, QSB, XPT[0], XPT[1]), (Kt, KSB, XPT[2], XPT[3])):
            tile = SLB[:, tt * 512:(tt + 1) * 512]
            ev = fl.emit("scalar", lambda e, tile=tile: e.activation(
                SC1, tile, AF.Square), sem="sc",
                waits=[pev, lam_l, xpt_l])
            ev = fl.emit("vector", lambda e: e.tensor_reduce(
                RN8, SC1.rearrange("p (h d) -> p h d", d=64), AX.X, ALU.add),
                sem="ev", waits=[ev])
            ev = fl.emit("vector", lambda e: e.reciprocal(RN8, RN8), sem="ev", waits=[ev])
            ev = fl.emit("scalar", lambda e: e.activation(RN8, RN8, AF.Sqrt),
                         sem="sc", waits=[ev])
            ev = fl.emit("vector", lambda e, tile=tile: e.tensor_mul(
                tile.rearrange("p (h d) -> p h d", d=64),
                tile.rearrange("p (h d) -> p h d", d=64),
                RN8.unsqueeze(2).to_broadcast([128, 8, 64])),
                sem="ev", waits=[ev])
            ev = fl.emit("vector", lambda e, tile=tile, SCB=SCB: e.tensor_mul(
                tile, tile, SCB), sem="ev", waits=[ev])
            EV = tile.rearrange("p (h j) -> p h j", j=64)[:, :, 0:32]
            OV = tile.rearrange("p (h j) -> p h j", j=64)[:, :, 32:64]
            ctv = CT.rearrange("p (a j) -> p a j", j=32)[:, tt:tt + 1, :].to_broadcast([128, 8, 32])
            stv = ST.rearrange("p (a j) -> p a j", j=32)[:, tt:tt + 1, :].to_broadcast([128, 8, 32])
            t1 = T128.rearrange("p (h j) -> p h j", j=64)[:, :, 0:32]
            t2 = T128.rearrange("p (h j) -> p h j", j=64)[:, :, 32:64]
            t3 = SC1.rearrange("p (h j) -> p h j", j=64)[:, :, 0:32]
            t4v = SC1.rearrange("p (h j) -> p h j", j=64)[:, :, 32:64]
            ev = fl.emit("vector", lambda e, EV=EV, ctv=ctv, t1=t1: e.tensor_mul(t1, EV, ctv),
                         sem="ev", waits=[ev])
            ev = fl.emit("vector", lambda e, OV=OV, stv=stv, t2=t2: e.tensor_mul(t2, OV, stv),
                         sem="ev", waits=[ev])
            ev = fl.emit("vector", lambda e, EV=EV, stv=stv, t3=t3: e.tensor_mul(t3, EV, stv),
                         sem="ev", waits=[ev])
            ev = fl.emit("vector", lambda e, OV=OV, ctv=ctv, t4v=t4v: e.tensor_mul(t4v, OV, ctv),
                         sem="ev", waits=[ev])
            ev = fl.emit("vector", lambda e, EV=EV, t1=t1, t2=t2: e.tensor_sub(EV, t1, t2), sem="ev", waits=[ev])
            ev = fl.emit("vector", lambda e, OV=OV, t3=t3, t4v=t4v: e.tensor_add(OV, t4v, t3), sem="ev", waits=[ev])
            pev = ev
        rope_ev[tt] = ev
    rope_done = rope_ev[-1]

    pb = [ps[2], ps[3]]
    i = 0
    cpv = None
    tr_cp = {}
    for SLB, DST in ((Q, QT), (Kt, KT)):
        for tt in range(ntt):
            for c4 in range(4):
                bank = pb[i % 2]
                trw = [rope_done] if i < 2 else [tr_cp[i - 2]]
                fl.emit("tensor", lambda e, SLB=SLB, tt=tt, c4=c4, bank=bank: e.matmul(
                    bank[:, 0:128], SLB[:, tt * 512 + c4 * 128:tt * 512 + (c4 + 1) * 128],
                    ident[:, :], start=True, stop=True),
                    sem="mm", waits=trw)
                tme = ("mm", fl.cnt["mm"])
                cpv = fl.emit("vector", lambda e, DST=DST, tt=tt, c4=c4, bank=bank: e.tensor_copy(
                    DST[:, c4 * T + tt * 128:c4 * T + tt * 128 + 128], bank[:, 0:128]),
                    sem="ev", waits=[tme] + ([cpv] if cpv else []))
                tr_cp[i] = cpv
                i += 1
    qkt_done = cpv

    NM2 = SCL2[:, 1408:1410]
    NMS = SCL2[:, 1410:1411]
    SM2 = SCL2[:, 1411:1413]
    SMT = SCL2[:, 1413:1414]
    RINV = SCL2[:, 1414:1415]
    prev_exp = None
    prev_pv = None
    prev_ao = None
    ao_ev = {}
    nw = T // 512
    for w in range(nw):
        for t4 in range(4):
            tt = w * 4 + t4
            for h in range(8):
                ch, b = h // 2, 64 * (h % 2)
                mmw = [qkt_done] + ([prev_exp] if prev_exp else [])
                for half in range(2):
                    wk = w - 1 + half
                    fl.emit("tensor", lambda e, tt=tt, h=h, ch=ch, b=b, half=half, wk=wk: e.matmul(
                        ps[half][:, 0:512],
                        QT[b:b + 64, ch * T + tt * 128:ch * T + tt * 128 + 128],
                        KT[b:b + 64, ch * T + max(wk, 0) * 512:ch * T + max(wk, 0) * 512 + 512],
                        start=True, stop=True),
                        sem=("mm" if half == 1 else None),
                        waits=(mmw if half == 0 else ()))
                sc_mm = ("mm", fl.cnt["mm"])
                var = t4 if w == 0 else 4 + t4
                mev = None
                for half in range(2):
                    mev = fl.emit("vector", lambda e, half=half, var=var: e.tensor_add(
                        ps[half][:, 0:512], ps[half][:, 0:512],
                        LAM[:, var * 1024 + half * 512:var * 1024 + (half + 1) * 512]),
                        sem="ev", waits=[sc_mm] if half == 0 else [])
                ev = mev
                for half in range(2):
                    ev = fl.emit("vector", lambda e, half=half: e.reduce_max(
                        NM2[:, half:half + 1], ps[half][:, 0:512], AX.X, negate=True),
                        sem="ev", waits=[ev])
                ev = fl.emit("vector", lambda e: e.tensor_reduce(
                    NMS, NM2, AX.X, ALU.min), sem="ev", waits=[ev])
                nms_ev = fl.emit("vector", lambda e: e.tensor_scalar(
                    NMS, NMS, 8.0, None, ALU.mult), sem="ev", waits=[ev])
                ee = None
                for half in range(2):
                    ee = fl.emit("scalar", lambda e, half=half: e.activation(
                        PR[:, half * 512:(half + 1) * 512], ps[half][:, 0:512], AF.Exp,
                        bias=NMS, scale=8.0, accum_out=SM2[:, half:half + 1]),
                        sem="sc", waits=[nms_ev] + ([ee] if ee else []))
                exp_ev = ee
                prev_exp = exp_ev
                ev = fl.emit("vector", lambda e: e.reduce_sum(SMT, SM2, AX.X),
                             sem="ev", waits=[exp_ev])
                rinv_ev = fl.emit("vector", lambda e: e.reciprocal(RINV, SMT),
                                  sem="ev", waits=[ev])
                cp8 = [None] * 8
                for t in range(8):
                    bank = pb[t % 2]
                    trw = [exp_ev]
                    if t >= 2:
                        trw.append(cp8[t - 2])
                    elif prev_pv:
                        trw.append(prev_pv)
                    fl.emit("tensor", lambda e, t=t, bank=bank: e.matmul(
                        bank[:, 0:128], PR[:, t * 128:(t + 1) * 128],
                        ident[:, :], start=True, stop=True),
                        sem="mm", waits=trw)
                    tme = ("mm", fl.cnt["mm"])
                    cp8[t] = fl.emit("vector", lambda e, t=t, bank=bank: e.tensor_copy(
                        PT[:, t * 128:(t + 1) * 128], bank[:, 0:128]),
                        sem="ev", waits=[tme] + ([prev_pv] if (prev_pv and t < 2) else []))
                pvw = [cp8[7]] + ([prev_ao] if prev_ao else [])
                kc0 = 4 if w == 0 else 0
                for kc in range(kc0, 8):
                    ktt = (w - 1) * 4 + kc
                    fl.emit("tensor", lambda e, kc=kc, h=h, ktt=ktt, kc0=kc0: e.matmul(
                        ps[4][:, 0:64],
                        PT[:, kc * 128:(kc + 1) * 128],
                        V[:, ktt * 512 + h * 64:ktt * 512 + h * 64 + 64],
                        start=(kc == kc0), stop=(kc == 7)),
                        sem=("mm" if kc == 7 else None),
                        waits=(pvw if kc == kc0 else ()))
                pv_mm = ("mm", fl.cnt["mm"])
                prev_pv = pv_mm
                ae = fl.emit("vector", lambda e, tt=tt, h=h: e.tensor_scalar(
                    AO[:, tt * 512 + h * 64:tt * 512 + h * 64 + 64],
                    ps[4][:, 0:64], RINV, None, ALU.mult),
                    sem="ev", waits=[pv_mm, rinv_ev] + ([prev_ao] if prev_ao else []))
                ao_ev[(tt, h)] = ae
                prev_ao = ae
    att_done = prev_ao

    cpv = None
    cp_list = {}
    i = 0
    for tt in range(ntt):
        for c4 in range(4):
            bank = pb[i % 2]
            trw = [att_done, prev_pv] if i < 2 else [cp_list[i - 2]]
            fl.emit("tensor", lambda e, tt=tt, c4=c4, bank=bank: e.matmul(
                bank[:, 0:128], AO[:, tt * 512 + c4 * 128:tt * 512 + (c4 + 1) * 128],
                ident[:, :], start=True, stop=True),
                sem="mm", waits=trw)
            tme = ("mm", fl.cnt["mm"])
            cpv = fl.emit("vector", lambda e, tt=tt, c4=c4, bank=bank: e.tensor_copy(
                AOT[:, c4 * T + tt * 128:c4 * T + tt * 128 + 128], bank[:, 0:128]),
                sem="ev", waits=[tme] + ([cpv] if cpv else []))
            cp_list[i] = cpv
            i += 1
    aot_done = cpv
    XRT = cx.regf(L2_XRT_O, T)
    XOT = cx.regf(L2_XRT_O + 2 * T, T)
    fw = [aot_done]
    for hv in range(2):
        tok0 = hv * th
        d_ev = _emit_stage_d(cx, cx.ap[pfx + "wop"][l], None, None, None, zres, tok0, th,
                             first_waits=fw, XRT=XRT, XOT=XOT, n_kc=4,
                             moving_fn=lambda kc, n, tok0=tok0: AOT[:, kc * T + tok0 + n * 512:kc * T + tok0 + (n + 1) * 512],
                             wr_region=cx.reg(L2_WR2_O, 3072), wr_base_off=0)
        fw = [d_ev[-1], d_ev[-2], ("ev", fl.cnt["ev"])]
    attn_res_done = fw[:2]
    fl.fence()

    lgv = SCL[:, 0:8]
    lbv = SCL[:, 8:16]
    b1av = SCL[:, 16:38]
    b1gv = SCL[:, 38:60]
    b2v = SCL[:, 60:68]
    sl1 = fl.dma("gpsimd", lgv, cx.ap[pfx + "lg"][l], "gs", waits=attn_res_done)
    sl2 = fl.dma("gpsimd", lbv, cx.ap[pfx + "lb"][l], "gs")
    sl3 = fl.dma("gpsimd", b1av, cx.ap[pfx + "b1a"][l], "gs")
    sl4 = fl.dma("gpsimd", b1gv, cx.ap[pfx + "b1g"][l], "gs")
    sl5 = fl.dma("gpsimd", b2v, cx.ap[pfx + "b2"][l], "gs")
    ln2 = emit_ln(cx, zres, gain_view=lgv, bias_view=lbv,
                  extra_wait=attn_res_done[0], shift_half=True)
    fl.wait("vector", sl1)
    fl.wait("vector", sl2)

    A = cx.reg(L2_A_O, 22 * th)
    G = cx.reg(L2_G_O, 22 * th)
    WR2 = cx.reg(L2_WR2_O, 6144)
    prev_half = None
    for hv in range(2):
        tok0 = hv * th
        a_ev = [None] * 22
        g_ev = [None] * 22
        mms = [None] * 22
        for oc in range(22):
            slot = oc % 3
            base = slot * 2048
            lww = [mms[oc - 3]] if oc >= 3 else ([ln2] if hv == 0 else [prev_half])
            lw1 = fl.dma("gpsimd", WR2[:, base:base + 1024].rearrange("p (k m) -> p k m", m=128),
                         cx.ap[pfx + "w1a"][l][oc].rearrange("k p m -> p k m"),
                         "wr%d" % slot, waits=lww)
            lw2 = fl.dma("gpsimd", WR2[:, base + 1024:base + 2048].rearrange("p (k m) -> p k m", m=128),
                         cx.ap[pfx + "w1g"][l][oc].rearrange("k p m -> p k m"),
                         "wr%d" % slot)
            grp = (oc % 2) * 4
            mmw = [lw1, lw2, ln2, sl3, sl4]
            if oc >= 2:
                mmw.append(g_ev[oc - 2])
                mmw.append(a_ev[oc - 2])
            last = None
            for w_, woff in ((0, 0), (1, 1024)):
                for n in range(nth):
                    for kc in range(DC):
                        last = fl.emit("tensor", lambda e, w_=w_, n=n, kc=kc, base=base, woff=woff, grp=grp, tok0=tok0: e.matmul(
                            ps[grp + w_ * 2 + n][:, 0:512],
                            WR2[:, base + woff + kc * 128:base + woff + (kc + 1) * 128],
                            XB[:, kc * T + tok0 + n * 512:kc * T + tok0 + (n + 1) * 512],
                            start=(kc == 0), stop=(kc == DC - 1)),
                            sem=("mm" if (w_ == 1 and n == nth - 1 and kc == DC - 1) else None),
                            waits=(mmw if (w_ == 0 and n == 0 and kc == 0) else ()))
            mms[oc] = ("mm", fl.cnt["mm"])
            ae = None
            for n in range(nth):
                ae = fl.emit("vector", lambda e, oc=oc, n=n, grp=grp: e.tensor_scalar(
                    A[:, oc * th + n * 512:oc * th + (n + 1) * 512],
                    ps[grp + n][:, 0:512], b1av[:, oc:oc + 1], None, ALU.add),
                    sem="ev", waits=([mms[oc]] + ([a_ev[oc - 1]] if oc >= 1 else [])) if n == 0 else [])
            a_ev[oc] = ae
            ge = None
            for n in range(nth):
                ge = fl.emit("scalar", lambda e, oc=oc, n=n, grp=grp: e.activation(
                    G[:, oc * th + n * 512:oc * th + (n + 1) * 512],
                    ps[grp + 2 + n][:, 0:512], AF.Gelu, bias=b1gv[:, oc:oc + 1]),
                    sem="sc", waits=([mms[oc]] + ([g_ev[oc - 1]] if oc >= 1 else [])) if n == 0 else [])
            g_ev[oc] = ge
        hev = None
        for oc in range(22):
            hev = fl.emit("vector", lambda e, oc=oc: e.tensor_mul(
                A[:, oc * th:(oc + 1) * th], A[:, oc * th:(oc + 1) * th],
                G[:, oc * th:(oc + 1) * th]),
                sem="ev", waits=[a_ev[-1], g_ev[-1]] if oc == 0 else [])
        h_done = hev
        fw2 = [h_done]
        if prev_half is not None:
            fw2 += [prev_half, prev_half2]
        d_ev = _emit_stage_d(cx, cx.ap[pfx + "w2p"][l], b2v, sl5,
                             A, zres, tok0, th,
                             first_waits=fw2, XRT=XRT, XOT=XOT, n_kc=22,
                             wr_region=cx.reg(L2_WR2_O, 6144), wr_base_off=0,
                             bias_mode=True, ring_slots=2)
        prev_half = d_ev[-1]
        prev_half2 = d_ev[-2]
    return prev_half



def emit_l1ffn(cx, pfx, alpha):
    fl, T = cx.fl, cx.T
    nt = T // 512
    ntt = T // 128
    CAP = T // 8
    ncall = CAP // 128
    ps = cx.ps
    XB = cx.reg(XB_O, DC * T)
    XNT = cx.reg(16384, ntt * 1024)
    Arows = cx.regf(40960, T, 0, 16)
    Ai = cx.slab[0:16, 40960:40960 + 2 * T].bitcast(I32)
    EArows = cx.regf(45056, T, 0, 16)
    EAi = cx.slab[0:16, 45056:45056 + 2 * T].bitcast(I32)
    WK = cx.regf(49152, T, 0, 16)
    WKi = cx.slab[0:16, 49152:49152 + 2 * T].bitcast(I32)
    SELf = cx.regf(53248, CAP, 0, 16)
    SELi = cx.slab[0:16, 53248:53248 + 2 * CAP].bitcast(I32)
    EIN = cx.reg(55296, 4096)
    ZR = cx.regf(55296, 1024)
    EINT = cx.reg(59392, 4096)
    XF = [cx.regf(63488, T), cx.regf(63488 + 2 * T, T)]
    HT = cx.reg(63488, 8192)
    EOT = cx.reg(71680, 2048)
    EOF = cx.regf(73728, 1024)
    WR = cx.reg(77824, 11264)
    xres = cx.ap["xres"]
    ident = cx.sb["IDENT"]
    SCL = cx.sb["SCL"]
    SCL2 = cx.sb["SCL2"]
    onecf = SCL2[:, 1499:1500]
    ONERF = SCL2[0:1, 1532:1660]
    xnsp = cx.ap["xnspill"]
    ybuf = cx.ap["ybuf"]
    idxb = cx.ap["idxbuf"]
    gatb = cx.ap["gatebuf"]

    trch = {"i": 0, "cp": {}}

    def ptrans(src, copy_fn, extra=(), copy_extra=()):
        i = trch["i"]
        bank = ps[6 + i % 2]
        w = [x for x in extra if x is not None]
        if i >= 2:
            w.append(trch["cp"][i - 2])
        fl.emit("tensor", lambda e, bank=bank, src=src: e.matmul(
            bank[:, 0:128], src, ident[:, :], start=True, stop=True),
            sem="mm", waits=w)
        tme = ("mm", fl.cnt["mm"])
        cp = fl.emit("vector", lambda e, bank=bank, copy_fn=copy_fn: copy_fn(e, bank),
                     sem="ev", waits=[tme] + [x for x in copy_extra if x is not None])
        trch["cp"][i] = cp
        trch["i"] = i + 1
        return cp

    wrch = {}

    def wdma(sem, out, in_, waits=()):
        w = [x for x in waits if x is not None]
        if sem in wrch:
            w.append(wrch[sem])
        ev = fl.dma("gpsimd", out, in_, sem, waits=w)
        wrch[sem] = ev
        return ev

    def windirect(sem, fn, waits=()):
        w = [x for x in waits if x is not None]
        if sem in wrch:
            w.append(wrch[sem])
        ev = fl.emit("gpsimd", fn, sem=sem, amt=16, waits=w)
        wrch[sem] = ev
        return ev

    ze = fl.emit("vector", lambda e: e.memset(ZR, 0.0), sem="ev")
    zb = None
    for tt in range(ntt):
        zb = fl.dma("sync", ybuf[tt * 128:(tt + 1) * 128, :], ZR, "xw",
                    waits=[ze] if tt == 0 else ())
    zb_done = ("xw", fl.cnt["xw"])

    ln_done = emit_ln(cx, xres)
    rf_o = S1_O + 4 * T
    a_row32 = cx.regf(rf_o + 2 * T, T, 0, 1)
    b_row32 = cx.regf(rf_o + 4 * T, T, 0, 1)

    GQF = SCL2[:, 1280:1408]
    gql = fl.dma("gpsimd", GQF.rearrange("p (k m) -> p k m", m=16),
                 cx.ap[pfx + "gqp"].rearrange("k p m -> p k m"), "gs")
    ltl = fl.dma("gpsimd", SCL2[0:1, 1408:1409], cx.ap[pfx + "lt"], "gs")
    xf_mm = [None, None]
    se_mm = None
    for kc in range(DC):
        par = kc % 2
        xfl = fl.dma("sync", XF[par], xres[kc * 128:(kc + 1) * 128, :],
                     "xd%d" % par, waits=[xf_mm[par]])
        for n in range(nt):
            fl.emit("tensor", lambda e, kc=kc, n=n, par=par: e.matmul(
                ps[n % 4][0:16, 0:512], GQF[:, kc * 16:(kc + 1) * 16],
                XF[par][:, n * 512:(n + 1) * 512],
                start=(kc == 0), stop=(kc == DC - 1)),
                sem=("mm" if n == nt - 1 else None),
                waits=([xfl, gql, ln_done] if n == 0 else ()))
        fl.emit("tensor", lambda e, kc=kc: e.matmul(
            ps[4][0:1, 0:16], onecf, GQF[:, kc * 16:(kc + 1) * 16],
            start=(kc == 0), stop=(kc == DC - 1)),
            sem="mm", waits=())
        se_mm = ("mm", fl.cnt["mm"])
        xf_mm[par] = se_mm
    raw_mm = se_mm
    ce = None
    for n in range(nt):
        ce = fl.emit("vector", lambda e, n=n: e.tensor_copy(
            Arows[0:16, n * 512:(n + 1) * 512], ps[n % 4][0:16, 0:512]),
            sem="ev", waits=[raw_mm] if n == 0 else [])
    ce = fl.emit("vector", lambda e: e.tensor_copy(
        SCL2[0:1, 1409:1425], ps[4][0:1, 0:16]), sem="ev", waits=[ce])
    fl.emit("tensor", lambda e: e.matmul(
        ps[5][0:16, 0:1], SCL2[0:1, 1409:1425], onecf[0:1, 0:1],
        start=True, stop=True), sem="mm", waits=[ce])
    tse = ("mm", fl.cnt["mm"])
    ev = fl.emit("vector", lambda e: e.tensor_copy(
        SCL2[0:16, 1425:1426], ps[5][0:16, 0:1]), sem="ev", waits=[tse])
    ev = fl.emit("scalar", lambda e: e.activation(
        SCL2[0:1, 1426:1427], SCL2[0:1, 1408:1409], AF.Exp), sem="sc",
        waits=[ltl, ev])
    ev = fl.emit("vector", lambda e: e.reciprocal(
        SCL2[0:1, 1427:1428], SCL2[0:1, 1426:1427]), sem="ev", waits=[ev])
    fl.emit("tensor", lambda e: e.matmul(
        ps[5][0:16, 0:1], ONERF[0:1, 0:16], SCL2[0:1, 1427:1428],
        start=True, stop=True), sem="mm", waits=[ev])
    tie = ("mm", fl.cnt["mm"])
    ev = fl.emit("vector", lambda e: e.tensor_copy(
        SCL2[0:16, 1428:1429], ps[5][0:16, 0:1]), sem="ev", waits=[tie])
    for n in range(nt):
        fl.emit("tensor", lambda e, n=n: e.matmul(
            ps[(n % 2) * 2][0:16, 0:512], ONERF[0:1, 0:16],
            a_row32[0:1, n * 512:(n + 1) * 512], start=True, stop=True),
            sem="mm", waits=[ev])
        abc = ("mm", fl.cnt["mm"])
        ev = fl.emit("vector", lambda e, n=n: e.tensor_tensor(
            Arows[0:16, n * 512:(n + 1) * 512], Arows[0:16, n * 512:(n + 1) * 512],
            ps[(n % 2) * 2][0:16, 0:512], ALU.mult), sem="ev", waits=[abc, ev])
        fl.emit("tensor", lambda e, n=n: e.matmul(
            ps[(n % 2) * 2 + 1][0:16, 0:512], ONERF[0:1, 0:16],
            b_row32[0:1, n * 512:(n + 1) * 512], start=True, stop=True),
            sem="mm", waits=[ev])
        bbc = ("mm", fl.cnt["mm"])
        ev = fl.emit("vector", lambda e, n=n: e.tensor_scalar(
            EArows[0:16, n * 512:(n + 1) * 512], ps[(n % 2) * 2 + 1][0:16, 0:512],
            SCL2[0:16, 1425:1426], None, ALU.mult), sem="ev", waits=[bbc, ev])
        ev = fl.emit("vector", lambda e, n=n: e.tensor_add(
            Arows[0:16, n * 512:(n + 1) * 512], Arows[0:16, n * 512:(n + 1) * 512],
            EArows[0:16, n * 512:(n + 1) * 512]), sem="ev", waits=[ev])
        ev = fl.emit("vector", lambda e, n=n: e.tensor_scalar(
            Arows[0:16, n * 512:(n + 1) * 512], Arows[0:16, n * 512:(n + 1) * 512],
            SCL2[0:16, 1428:1429], None, ALU.mult), sem="ev", waits=[ev])
    affine_done = ev

    for it in range(6):
        ee = fl.emit("scalar", lambda e: e.activation(EArows, Arows, AF.Exp),
                     sem="sc", waits=[ev])
        last = None
        for n in range(nt):
            fl.emit("tensor", lambda e, n=n: e.matmul(
                ps[4 + n % 2][0:1, 0:512], onecf[0:16, 0:1],
                EArows[0:16, n * 512:(n + 1) * 512], start=True, stop=True),
                sem="mm", waits=[ee] + ([last] if last else []))
            smm = ("mm", fl.cnt["mm"])
            le = fl.emit("scalar", lambda e, n=n: e.activation(
                EArows[0:1, n * 512:(n + 1) * 512], ps[4 + n % 2][0:1, 0:512],
                AF.Ln), sem="sc", waits=[smm])
            fl.emit("tensor", lambda e, n=n: e.matmul(
                ps[4 + n % 2][0:16, 0:512], ONERF[0:1, 0:16],
                EArows[0:1, n * 512:(n + 1) * 512], start=True, stop=True),
                sem="mm", waits=[le])
            bmm = ("mm", fl.cnt["mm"])
            last = fl.emit("vector", lambda e, n=n: e.tensor_tensor(
                Arows[0:16, n * 512:(n + 1) * 512], Arows[0:16, n * 512:(n + 1) * 512],
                ps[4 + n % 2][0:16, 0:512], ALU.subtract), sem="ev", waits=[bmm])
        ev = last
        ee = fl.emit("scalar", lambda e: e.activation(EArows, Arows, AF.Exp),
                     sem="sc", waits=[ev])
        ev = fl.emit("vector", lambda e: e.reduce_sum(
            SCL2[0:16, 1429:1430], EArows, AX.X), sem="ev", waits=[ee])
        ev = fl.emit("scalar", lambda e: e.activation(
            SCL2[0:16, 1430:1431], SCL2[0:16, 1429:1430], AF.Ln), sem="sc",
            waits=[ev])
        ev = fl.emit("vector", lambda e: e.tensor_scalar(
            Arows, Arows, SCL2[0:16, 1430:1431], None, ALU.subtract),
            sem="ev", waits=[ev])
    aff_ev = fl.emit("scalar", lambda e: e.activation(EArows, Arows, AF.Exp),
                     sem="sc", waits=[ev])

    iotl = fl.dma("gpsimd", WKi, cx.ap[pfx + "iota"], "gs")
    ev = fl.emit("vector", lambda e: e.tensor_scalar(
        Ai, EAi, ~0x7FF, None, ALU.bitwise_and), sem="ev", waits=[aff_ev, iotl])
    ev = fl.emit("vector", lambda e: e.tensor_tensor(
        WKi, Ai, WKi, ALU.bitwise_or), sem="ev", waits=[ev])
    M8L = SCL2[0:16, 1431:1439]
    for r in range(CAP // 8):
        ev = fl.emit("vector", lambda e: e.max(out=M8L, in_=WK), sem="ev",
                     waits=[ev])
        ev = fl.emit("vector", lambda e, r=r: e.tensor_copy(
            SELf[0:16, r * 8:(r + 1) * 8], M8L), sem="ev", waits=[ev])
        ev = fl.emit("vector", lambda e: e.match_replace(
            out=WK, in_to_replace=M8L, in_values=WK, imm_value=0.0),
            sem="ev", waits=[ev])
    ev = fl.emit("vector", lambda e: e.tensor_scalar(
        EAi[0:16, 0:CAP], SELi, 0x7FF, None, ALU.bitwise_and),
        sem="ev", waits=[ev])
    idx_sp = fl.dma("gpsimd", idxb.rearrange("(e c) -> e c", c=CAP),
                    EAi[0:16, 0:CAP], "gs", waits=[ev])
    ev = fl.emit("vector", lambda e: e.tensor_scalar(
        SELi, SELi, ~0x7FF, None, ALU.bitwise_and), sem="ev", waits=[ev])
    ev = fl.emit("vector", lambda e, alpha=alpha: e.tensor_scalar(
        SELf, SELf, float(alpha), None, ALU.mult), sem="ev", waits=[ev])
    gat_sp = fl.dma("gpsimd", gatb.rearrange("(e c) -> e c", c=CAP), SELf,
                    "gs", waits=[ev])

    for tt in range(ntt):
        for c in range(DC):
            cp = ptrans(XB[:, c * T + tt * 128:c * T + tt * 128 + 128],
                        lambda e, bank, tt=tt, c=c: e.tensor_copy(
                            XNT[:, tt * 1024 + c * 128:tt * 1024 + (c + 1) * 128],
                            bank[:, 0:128]),
                        extra=[affine_done] if trch["i"] < 2 else ())
    sp_ev = [None] * ntt
    for tt in range(ntt):
        sp_ev[tt] = fl.dma("gpsimd", xnsp[tt * 128:(tt + 1) * 128, :],
                           XNT[:, tt * 1024:(tt + 1) * 1024], "xc%d" % (tt % 8),
                           waits=[cp if tt < 8 else sp_ev[tt - 8]])

    w1p = cx.ap[pfx + "w1p"]
    w2p = cx.ap[pfx + "w2p"]
    pf1 = [fl.dma("gpsimd", WR[:, s * 1024:(s + 1) * 1024].rearrange(
        "p (k m) -> p k m", m=128), w1p[0][s].rearrange("k p m -> p k m"),
        "wr%d" % s) for s in range(3)]
    pf2 = [fl.dma("gpsimd", WR[:, 3072 + s * 4096:3072 + (s + 1) * 4096].rearrange(
        "p (k m) -> p k m", m=128), w2p[0][s].rearrange("k p m -> p k m"),
        "wr%d" % (6 + s)) for s in range(2)]
    fl.fence()

    eint_done = {}
    prev_w1mm = {}
    prev_w2mm = {}
    prev_hlast = {}
    prev_eolast = {}
    prev_sc = None
    for e_ in range(16):
        buf = e_ % 2
        wrs = "wr%d" % (4 + buf)
        b1v = SCL[:, buf * 40:buf * 40 + 32]
        b2v = SCL[:, buf * 40 + 32:buf * 40 + 40]
        g_ev = []
        for g in range(ncall):
            icol = SCL2[:, 1440 + buf * 2 + g:1441 + buf * 2 + g].bitcast(I32)
            st = (e_ * ncall + g) * 128
            lix = wdma(wrs, icol, idxb.unsqueeze(1)[st:st + 128, :],
                       waits=[eint_done.get(e_ - 2)])
            gth = windirect(wrs, lambda e, g=g, buf=buf, icol=icol: e.indirect_dma_start(
                EIN[:, buf * 2048 + g * 1024:buf * 2048 + (g + 1) * 1024],
                None, xnsp, IndirectOffsetOnAxis(icol, 0)),
                waits=[lix])
            g_ev.append(gth)
        cp = None
        for g in range(ncall):
            for c in range(DC):
                ex = [g_ev[g]]
                if g == 0 and c == 0:
                    ex.append(prev_w1mm.get(e_ - 2))
                cp = ptrans(EIN[:, buf * 2048 + g * 1024 + c * 128:
                                buf * 2048 + g * 1024 + (c + 1) * 128],
                            lambda e, bank, g=g, c=c, buf=buf: e.tensor_copy(
                                EINT[:, buf * 2048 + c * 256 + g * 128:
                                     buf * 2048 + c * 256 + (g + 1) * 128],
                                bank[:, 0:128]),
                            extra=ex)
        eint_done[e_] = cp
        sb1 = fl.dma("gpsimd", b1v, cx.ap[pfx + "b1s"][e_], "gs",
                     waits=[prev_hlast.get(e_ - 2)])
        sb2 = fl.dma("gpsimd", b2v, cx.ap[pfx + "b2s"][e_], "gs",
                     waits=[prev_eolast.get(e_ - 2)])
        h_ev = [None] * 32
        w1mm = [None] * 32
        for oc in range(32):
            slot = oc % 3
            if e_ == 0 and oc < 3:
                lw = pf1[oc]
            else:
                lww = [w1mm[oc - 3]] if oc >= 3 else [prev_w1mm.get(e_ - 1)]
                lw = fl.dma("gpsimd",
                            WR[:, slot * 1024:(slot + 1) * 1024].rearrange(
                                "p (k m) -> p k m", m=128),
                            w1p[e_][oc].rearrange("k p m -> p k m"),
                            "wr%d" % slot, waits=lww)
            bank = ps[oc % 4]
            mmw = [lw, eint_done[e_], sb1]
            if oc >= 4:
                mmw.append(h_ev[oc - 4])
            for kc in range(DC):
                fl.emit("tensor", lambda e, oc=oc, kc=kc, bank=bank, buf=buf,
                        slot=slot: e.matmul(
                    bank[:, 0:CAP],
                    WR[:, slot * 1024 + kc * 128:slot * 1024 + (kc + 1) * 128],
                    EINT[:, buf * 2048 + kc * 256:buf * 2048 + kc * 256 + CAP],
                    start=(kc == 0), stop=(kc == DC - 1)),
                    sem=("mm" if kc == DC - 1 else None),
                    waits=(mmw if kc == 0 else ()))
            w1mm[oc] = ("mm", fl.cnt["mm"])
            h_ev[oc] = fl.emit("scalar", lambda e, oc=oc, bank=bank, b1v=b1v: e.activation(
                HT[:, oc * 256:oc * 256 + CAP], bank[:, 0:CAP],
                AF.Gelu_apprx_tanh, bias=b1v[:, oc:oc + 1]),
                sem="sc", waits=[w1mm[oc]])
        prev_w1mm[e_] = w1mm[-1]
        prev_hlast[e_] = h_ev[-1]
        eo_ev = [None] * DC
        w2mm = [None] * DC
        for oc in range(DC):
            slot = oc % 2
            base = 3072 + slot * 4096
            if e_ == 0 and oc < 2:
                lw = pf2[oc]
            else:
                lww = [w2mm[oc - 2]] if oc >= 2 else [prev_w2mm.get(e_ - 1)]
                lw = fl.dma("gpsimd",
                            WR[:, base:base + 4096].rearrange("p (k m) -> p k m", m=128),
                            w2p[e_][oc].rearrange("k p m -> p k m"),
                            "wr%d" % (6 + slot), waits=lww)
            bank = ps[4 + oc % 2]
            mmw = [lw, sb2, h_ev[-1]]
            if oc >= 2:
                mmw.append(eo_ev[oc - 2])
            for kc in range(32):
                fl.emit("tensor", lambda e, oc=oc, kc=kc, bank=bank, base=base: e.matmul(
                    bank[:, 0:CAP], WR[:, base + kc * 128:base + (kc + 1) * 128],
                    HT[:, kc * 256:kc * 256 + CAP],
                    start=(kc == 0), stop=(kc == 31)),
                    sem=("mm" if kc == 31 else None),
                    waits=(mmw if kc == 0 else ()))
            w2mm[oc] = ("mm", fl.cnt["mm"])
            eo_ev[oc] = fl.emit("vector", lambda e, oc=oc, bank=bank, b2v=b2v: e.tensor_scalar(
                EOT[:, oc * 256:oc * 256 + CAP], bank[:, 0:CAP],
                b2v[:, oc:oc + 1], None, ALU.add),
                sem="ev", waits=[w2mm[oc]])
        prev_w2mm[e_] = w2mm[-1]
        prev_eolast[e_] = eo_ev[-1]
        for g in range(ncall):
            gcol = SCL2[:, 1444 + buf * 2 + g:1445 + buf * 2 + g]
            st = (e_ * ncall + g) * 128
            glw = wdma(wrs, gcol, gatb.unsqueeze(1)[st:st + 128, :],
                       waits=[gat_sp, eo_ev[-1]])
            cp = None
            for c in range(DC):
                ex = [eo_ev[-1], glw]
                if c == 0:
                    ex.append(prev_sc)
                cp = ptrans(EOT[:, c * 256 + g * 128:c * 256 + (g + 1) * 128],
                            lambda e, bank, c=c, gcol=gcol: e.tensor_scalar(
                                EOF[:, c * 128:(c + 1) * 128], bank[:, 0:128],
                                gcol, None, ALU.mult),
                            extra=ex)
            icol = SCL2[:, 1440 + buf * 2 + g:1441 + buf * 2 + g].bitcast(I32)
            sc = windirect(wrs, lambda e, icol=icol: e.indirect_dma_start(
                ybuf, IndirectOffsetOnAxis(icol, 0), EOF, None,
                compute_op=ALU.add),
                waits=[cp, prev_sc, zb_done])
            prev_sc = sc
    fl.fence()

    YTILE = cx.reg(16384, 256)
    YTF = cx.regf(16768, T)
    XRT = [cx.regf(20992, T), cx.regf(20992 + 2 * T, T)]
    xcch = {}
    st_ev = [None] * DC
    tile_cp = {}
    ti = 0
    ad_prev = None
    for c in range(DC):
        xl = fl.dma("sync", XRT[c % 2], xres[c * 128:(c + 1) * 128, :],
                    "xd%d" % (c % 2), waits=[st_ev[c - 2]] if c >= 2 else ())
        cp = None
        for tt in range(ntt):
            tile = YTILE[:, (tt % 2) * 128:(tt % 2) * 128 + 128]
            ldw = [xcch.get(tt % 8), tile_cp.get(ti - 2)]
            ldy = fl.dma("gpsimd", tile,
                         ybuf[tt * 128:(tt + 1) * 128, c * 128:(c + 1) * 128],
                         "xc%d" % (tt % 8), waits=ldw)
            xcch[tt % 8] = ldy
            cp = ptrans(tile, lambda e, bank, tt=tt: e.tensor_copy(
                YTF[:, tt * 128:(tt + 1) * 128], bank[:, 0:128]),
                extra=[ldy],
                copy_extra=[ad_prev] if tt == 0 else ())
            tile_cp[ti] = cp
            ti += 1
        ad = fl.emit("vector", lambda e, c=c: e.tensor_add(
            XRT[c % 2], XRT[c % 2], YTF), sem="ev", waits=[cp, xl])
        ad_prev = ad
        st_ev[c] = fl.dma("sync", xres[c * 128:(c + 1) * 128, :], XRT[c % 2],
                          "xs%d" % (c % 2), waits=[ad])
    fl.fence()
    return st_ev[-1]



def emit_l2attn(cx, pfx, mock_cc=False):
    fl, T = cx.fl, cx.T
    ntt = T // 128
    ps = cx.ps
    XB = cx.reg(XB_O, DC * T)
    QTM = cx.reg(16384, 16384)
    KST = [cx.reg(32768, 1024), cx.reg(33792, 1024)]
    VST = [cx.reg(34816, 1024), cx.reg(35840, 1024)]
    WS = [cx.reg(40960, 8192), cx.reg(49152, 8192), cx.reg(57344, 8192)]
    AOT = cx.reg(40960, 16384)
    WRW = cx.reg(57344, 3072)
    XRT = cx.regf(60416, T)
    XOT = cx.regf(64512, T)
    PROD = cx.regf(68608, 1024)
    ACC = cx.regf(70656, 1024)
    KR = [cx.regf(72704, 512), cx.regf(73728, 512)]
    VR = [cx.regf(74752, 512), cx.regf(75776, 512)]
    KRb = [cx.reg(72704, 1024), cx.reg(73728, 1024)]
    VRb = [cx.reg(74752, 1024), cx.reg(75776, 1024)]
    AO = cx.reg(76800, 16384)
    BB = [cx.regf(93184, 1024), cx.regf(95232, 1024), cx.regf(97280, 1024)]
    BROW = cx.regf(99328, 1024, 0, 1)
    SCL = cx.sb["SCL"]
    SCL2 = cx.sb["SCL2"]
    ONERF = SCL2[0:1, 1532:1660]
    ident = cx.sb["IDENT"]
    xres = cx.ap["xres"]
    cin = cx.ap["cin_kv"]
    cout = cx.ap["cout_kv"]
    SCf = SCL2[:, 0:128]
    scv = SCL2[:, 0:128].rearrange("p (h r) -> p h r", r=8)
    MX = SCL2[:, 128:144]
    SM = SCL2[:, 144:160]
    RC = SCL2[:, 160:176]
    ABT = SCL2[:, 176:240].bitcast(BF16)
    ABv = ABT.rearrange("p (h r) -> p h r", r=8)

    lng = SCL[:, 0:8]
    lnb = SCL[:, 8:16]
    bov = SCL[:, 16:24]
    l1 = fl.dma("gpsimd", lng, cx.ap[pfx + "lng"], "gs")
    l2 = fl.dma("gpsimd", lnb, cx.ap[pfx + "lnb"], "gs")
    l3 = fl.dma("gpsimd", bov, cx.ap[pfx + "bo"], "gs")
    wl = []
    for i, nm in enumerate(("wq", "wk", "wv")):
        wl.append(fl.dma("gpsimd", WS[i].rearrange("p (k m) -> p k m", m=1024),
                         cx.ap[pfx + nm].rearrange("k p m -> p k m"),
                         "wr%d" % (3 + i)))
    bev = None
    bcp = None
    for j in range(3):
        lb = fl.dma("sync", BROW, cx.ap[pfx + "bqkv"][j:j + 1, :], "xd0",
                    waits=[bcp])
        for n in range(2):
            fl.emit("tensor", lambda e, j=j, n=n: e.matmul(
                ps[6 + n][:, 0:512], ONERF[0:1, 0:128],
                BROW[0:1, n * 512:(n + 1) * 512], start=True, stop=True),
                sem="mm", waits=[lb] + ([bcp] if bcp else []))
            bmm = ("mm", fl.cnt["mm"])
            bcp = fl.emit("vector", lambda e, j=j, n=n: e.tensor_copy(
                BB[j][:, n * 512:(n + 1) * 512], ps[6 + n][:, 0:512]),
                sem="ev", waits=[bmm])
    bias_done = bcp

    zres = cx.ap["zres"]
    ln1 = emit_ln(cx, xres)
    sp = None
    for c in range(DC):
        sp = fl.dma("gpsimd", zres[c * 128:(c + 1) * 128, :],
                    XB[:, c * T:(c + 1) * T], "xl", waits=[ln1])
    XRA = cx.regf(76800, T)
    ZRA = cx.regf(76800 + 2 * T, T)
    preadd_st = [None] * DC
    pa = None
    for c in range(DC):
        lx2 = fl.dma("sync", XRA, xres[c * 128:(c + 1) * 128, :], "ax0",
                     waits=[preadd_st[c - 1]] if c >= 1 else ())
        lz2 = fl.dma("sync", ZRA, zres[c * 128:(c + 1) * 128, :], "ax1",
                     waits=[sp] + ([pa] if pa else []))
        pa = fl.emit("vector", lambda e: e.tensor_add(XRA, XRA, ZRA),
                     sem="ev", waits=[lx2, lz2] + ([pa] if pa else []))
        preadd_st[c] = fl.dma("sync", xres[c * 128:(c + 1) * 128, :], XRA,
                              "ax2", waits=[pa])
    preadd_done = preadd_st[-1]
    ln_done = emit_ln(cx, zres, gain_view=lng, bias_view=lnb, extra_wait=sp)

    kv_ev = {}
    st_ev = {}
    for tt in range(ntt):
        buf = tt % 2
        for wi, STt, wsem in ((1, KST, "xs"), (2, VST, "xd")):
            ee = None
            for j in range(2):
                bank = ps[2 * wi - 2 + j]
                mmw = [ln_done, wl[wi], kv_ev.get((tt - 1, wi))]
                for kc in range(DC):
                    fl.emit("tensor", lambda e, tt=tt, wi=wi, j=j, kc=kc, bank=bank: e.matmul(
                        bank[:, 0:512], XB[:, kc * T + tt * 128:kc * T + (tt + 1) * 128],
                        WS[wi][:, kc * 1024 + j * 512:kc * 1024 + (j + 1) * 512],
                        start=(kc == 0), stop=(kc == DC - 1)),
                        sem=("mm" if kc == DC - 1 else None),
                        waits=(mmw if kc == 0 else ()))
                mme = ("mm", fl.cnt["mm"])
                ee = fl.emit("vector", lambda e, tt=tt, wi=wi, j=j, bank=bank, buf=buf,
                             STt=STt: e.tensor_tensor(
                    STt[buf][:, j * 512:(j + 1) * 512], bank[:, 0:512],
                    BB[wi][:, j * 512:(j + 1) * 512], ALU.add),
                    sem="ev", waits=[mme, bias_done, st_ev.get((tt - 2, wi))])
            kv_ev[(tt, wi)] = ee
            st_ev[(tt, wi)] = fl.dma(
                "sync", cin[(wi - 1) * T + tt * 128:(wi - 1) * T + (tt + 1) * 128, :],
                STt[buf], "%s%d" % (wsem, buf), waits=[ee])
    cc_waits = [st_ev[(ntt - 1, 1)], st_ev[(ntt - 2, 1)],
                st_ev[(ntt - 1, 2)], st_ev[(ntt - 2, 2)]]
    if mock_cc:
        for r in range(8):
            s = r * 128
            for w in range(2):
                cc_ev = fl.dma("gpsimd", cout[r * 2 * T + w * T:r * 2 * T + w * T + (T - s), :],
                               cin[w * T + s:w * T + T, :], "cc", waits=cc_waits)
                if s:
                    cc_ev = fl.dma("gpsimd",
                                   cout[r * 2 * T + w * T + (T - s):r * 2 * T + (w + 1) * T, :],
                                   cin[w * T:w * T + s, :], "cc", waits=cc_waits)
    else:
        cc_ev = fl.emit("gpsimd", lambda e: e.collective_compute(
            "AllGather", ALU.bypass, [list(range(8))], [cin], [cout]),
            sem="cc", amt=1, waits=cc_waits)

    q_ev = {}
    for tt in range(ntt):
        for j in range(2):
            bank = ps[4 + j]
            mmw = [ln_done, wl[0], q_ev.get(tt - 1)]
            for kc in range(DC):
                fl.emit("tensor", lambda e, tt=tt, j=j, kc=kc, bank=bank: e.matmul(
                    bank[:, 0:512], XB[:, kc * T + tt * 128:kc * T + (tt + 1) * 128],
                    WS[0][:, kc * 1024 + j * 512:kc * 1024 + (j + 1) * 512],
                    start=(kc == 0), stop=(kc == DC - 1)),
                    sem=("mm" if kc == DC - 1 else None),
                    waits=(mmw if kc == 0 else ()))
            mme = ("mm", fl.cnt["mm"])
            q_ev[tt] = fl.emit("vector", lambda e, tt=tt, j=j, bank=bank: e.tensor_tensor(
                QTM[:, tt * 1024 + j * 512:tt * 1024 + (j + 1) * 512], bank[:, 0:512],
                BB[0][:, j * 512:(j + 1) * 512], ALU.add),
                sem="ev", waits=[mme, bias_done])

    trch = {"i": 0, "cp": {}}

    def ptrans(src, copy_fn, extra=()):
        i = trch["i"]
        bank = ps[6 + i % 2]
        w = [x for x in extra if x is not None]
        if i >= 2:
            w.append(trch["cp"][i - 2])
        fl.emit("tensor", lambda e, bank=bank, src=src: e.matmul(
            bank[:, 0:128], src, ident[:, :], start=True, stop=True),
            sem="mm", waits=w)
        tme = ("mm", fl.cnt["mm"])
        cp = fl.emit("vector", lambda e, bank=bank, copy_fn=copy_fn: copy_fn(e, bank),
                     sem="ev", waits=[tme])
        trch["cp"][i] = cp
        trch["i"] = i + 1
        return cp

    xcch = {}
    kr_reader = {}
    vr_reader = {}
    half_done = [None, None]
    ao_prev = preadd_done
    for tt in range(ntt):
        pr = ao_prev
        for r in range(8):
            lk = fl.dma("gpsimd", KRb[r % 2],
                        cout[r * 2 * T + tt * 128:r * 2 * T + (tt + 1) * 128, :],
                        "xc%d" % r,
                        waits=[cc_ev, xcch.get(r), kr_reader.get(r % 2)])
            xcch[r] = lk
            pr = fl.emit("vector", lambda e, tt=tt, r=r: e.tensor_tensor(
                PROD, QTM[:, tt * 1024:(tt + 1) * 1024], KRb[r % 2], ALU.mult),
                sem="ev", waits=[lk, q_ev[tt]] + ([pr] if pr else []))
            kr_reader[r % 2] = pr
            pr = fl.emit("vector", lambda e, r=r: e.reduce_sum(
                scv[:, :, r:r + 1], PROD.rearrange("p (h d) -> p h d", d=64),
                AX.X), sem="ev", waits=[pr])
        ev = fl.emit("vector", lambda e: e.reduce_max(MX, scv, AX.X),
                     sem="ev", waits=[pr])
        ev = fl.emit("vector", lambda e: e.tensor_tensor(
            scv, scv, MX.unsqueeze(2).to_broadcast([128, 16, 8]), ALU.subtract),
            sem="ev", waits=[ev])
        ev = fl.emit("scalar", lambda e: e.activation(
            SCf, SCf, AF.Exp, scale=0.125), sem="sc", waits=[ev])
        ev = fl.emit("vector", lambda e: e.reduce_sum(SM, scv, AX.X),
                     sem="ev", waits=[ev])
        ab_ev = fl.emit("vector", lambda e: e.reciprocal(RC, SM), sem="ev",
                        waits=[ev])
        pv = ab_ev
        for r in range(8):
            lv = fl.dma("gpsimd", VRb[r % 2],
                        cout[(2 * r + 1) * T + tt * 128:(2 * r + 1) * T + (tt + 1) * 128, :],
                        "xc%d" % r,
                        waits=[cc_ev, xcch.get(r), vr_reader.get(r % 2)])
            xcch[r] = lv
            dst = ACC if r == 0 else PROD
            pv = fl.emit("vector", lambda e, r=r, dst=dst: e.tensor_tensor(
                dst.rearrange("p (h d) -> p h d", d=64),
                scv[:, :, r:r + 1].to_broadcast([128, 16, 64]),
                VRb[r % 2].rearrange("p (h d) -> p h d", d=64), ALU.mult),
                sem="ev", waits=[lv, ab_ev] + ([pv] if pv else []))
            vr_reader[r % 2] = pv
            if r > 0:
                pv = fl.emit("vector", lambda e: e.tensor_add(ACC, ACC, PROD),
                             sem="ev", waits=[pv])
        pv = fl.emit("vector", lambda e: e.tensor_tensor(
            ACC.rearrange("p (h d) -> p h d", d=64),
            ACC.rearrange("p (h d) -> p h d", d=64),
            RC.unsqueeze(2).to_broadcast([128, 16, 64]), ALU.mult),
            sem="ev", waits=[pv])
        ao_cp = fl.emit("vector", lambda e, tt=tt: e.tensor_copy(
            AO[:, tt * 1024:(tt + 1) * 1024], ACC), sem="ev", waits=[pv])
        ao_prev = ao_cp
        cpl = None
        for c in range(DC):
            ex = [ao_cp]
            if trch["i"] < 2:
                ex.append(bias_done)
            cpl = ptrans(AO[:, tt * 1024 + c * 128:tt * 1024 + (c + 1) * 128],
                         lambda e, bank, tt=tt, c=c: e.tensor_copy(
                             AOT[:, c * T + tt * 128:c * T + (tt + 1) * 128],
                             bank[:, 0:128]),
                         extra=ex)
        if tt == ntt // 2 - 1:
            half_done[0] = cpl
        elif tt == ntt - 1:
            half_done[1] = cpl

    bol = ("gs", fl.cnt["gs"])
    wop = cx.ap[pfx + "wop"]
    th = T // 2
    prev = None
    for hh in range(2):
        fw = [half_done[hh], kv_ev[(ntt - 1, 1)], kv_ev[(ntt - 1, 2)],
              q_ev[ntt - 1], preadd_done]
        if prev is not None:
            fw += [prev[-1], prev[-2]]
        d_ev = _emit_stage_d(
            cx, wop, bov, bol, None, xres, hh * th, th, fw, XRT, XOT,
            n_kc=DC,
            moving_fn=lambda kc, n, hh=hh: AOT[:, kc * T + hh * th + n * 512:
                                               kc * T + hh * th + (n + 1) * 512],
            wr_base_off=0, wr_region=WRW, bias_mode=True, ring_slots=3)
        prev = d_ev
    fl.fence()
    return prev[-1]



TPROD = 2048
FUSED = False


def _prep_all(inputs):
    T = TPROD
    g = lambda k: np.ascontiguousarray(np.asarray(inputs[k], np.float32))
    packs = {}

    def add(name, arr):
        arr = np.ascontiguousarray(arr.astype(np.float32, copy=False)
                                   if arr.dtype != np.int32 else arr)
        packs[name] = arr

    add("ident", np.eye(128, dtype=np.float32))

    for nm, key in (("w1p", "L0_ffn_w1_w"), ("w2p", "L0_ffn_w2_w"),
                    ("h1p", "L0_ffn_h1_w"), ("h2p", "L0_ffn_h2_w"),
                    ("w3p", "L0_ffn_w3_w")):
        add("l0f_" + nm, pack_lhsT_out_in(g(key)))
    for nm, key in (("s1", "L0_ffn_w1_s"), ("s2", "L0_ffn_w2_s"),
                    ("h1s", "L0_ffn_h1_s"), ("h2s", "L0_ffn_h2_s"),
                    ("s3", "L0_ffn_w3_s")):
        add("l0f_" + nm, pack_scale(g(key)))

    perm = np.concatenate([np.arange(0, D, 2), np.arange(1, D, 2)])
    add("l0a_wqp", pack_lhsT_out_in(g("L0_attn_wq")[perm]))
    add("l0a_wkp", pack_lhsT_out_in(g("L0_attn_wk")[perm]))
    add("l0a_wvT", g("L0_attn_wv").T.reshape(8, 128, D))
    add("l0a_wop", pack_lhsT_out_in(g("L0_attn_wo")))
    inv = 1.0 / (10000.0 ** (np.arange(0, D, 2, dtype=np.float32) / D))
    ang = np.outer(np.arange(T, dtype=np.float32), inv)
    add("l0a_ropeT", np.concatenate(
        [np.ascontiguousarray(np.cos(ang).T).reshape(4, 128, T),
         np.ascontiguousarray(np.sin(ang).T).reshape(4, 128, T)], 0))

    wqkv = g("L1_attn_W_qkv")
    add("l1a_wqp", pack_lhsT_in_out(np.ascontiguousarray(wqkv[:, 0:D])))
    add("l1a_wkp", pack_lhsT_in_out(np.ascontiguousarray(wqkv[:, D:2 * D])))
    add("l1a_wvT", np.ascontiguousarray(wqkv[:, 2 * D:3 * D]).reshape(8, 128, D))
    cmask = np.zeros((4, 128, 512), np.float32)
    for o in range(4):
        r = np.arange(128)[:, None]
        c = np.arange(512)[None, :]
        cmask[o] = np.where(c > o * 128 + r, -1e30, 0.0).astype(np.float32)
    add("l1a_cmask", cmask)

    add("l1f_gqp", g("L1_ffn_gate_query").T.reshape(8, 128, 16))
    add("l1f_lt", np.asarray(inputs["L1_ffn_log_temp"],
                             np.float32).reshape(1, 1))
    packs["l1f_iota"] = np.broadcast_to(
        np.arange(T, dtype=np.int32), (16, T)).copy()
    w1 = g("L1_ffn_w1")
    w2 = g("L1_ffn_w2")
    add("l1f_w1p", np.stack([pack_lhsT_in_out(w1[e]) for e in range(16)]))
    add("l1f_w2p", np.stack([pack_lhsT_in_out(w2[e]) for e in range(16)]))
    b1 = g("L1_ffn_b1")
    b2 = g("L1_ffn_b2")
    add("l1f_b1s", np.stack([pack_scale(b1[e]) for e in range(16)]))
    add("l1f_b2s", np.stack([pack_scale(b2[e]) for e in range(16)]))

    add("l2a_lng", pack_scale(g("L2_attn_ln_g")))
    add("l2a_lnb", pack_scale(g("L2_attn_ln_b")))
    add("l2a_bo", pack_scale(g("L2_attn_bo")))
    add("l2a_wq", g("L2_attn_Wq").reshape(8, 128, D))
    add("l2a_wk", g("L2_attn_Wk").reshape(8, 128, D))
    add("l2a_wv", g("L2_attn_Wv").reshape(8, 128, D))
    add("l2a_bqkv", np.stack([g("L2_attn_bq"), g("L2_attn_bk"),
                              g("L2_attn_bv")]))
    add("l2a_wop", pack_lhsT_in_out(g("L2_attn_Wo")))

    HID = 2730
    DH2, W2W = 64, 512
    half = np.arange(0, DH2, 2, dtype=np.float32)
    invf = 1.0 / 10000.0 ** (half / DH2)
    tt_ = np.arange(T, dtype=np.float32)
    fr = tt_[:, None] * invf[None, :]
    base = (half + 0.4 * DH2) / (1.4 * DH2)
    power = (tt_ - T // 2) / float(W2W // 2)
    sc = base[None, :] ** power[:, None]
    cosf, sinf = np.cos(fr), np.sin(fr)

    def tok_tiles(a):
        return np.ascontiguousarray(
            a.reshape(T // 128, 128, 32).transpose(1, 0, 2).reshape(128, -1))
    add("l2f_xpt", np.stack([tok_tiles(cosf * sc), tok_tiles(sinf * sc),
                             tok_tiles(cosf / sc), tok_tiles(sinf / sc)], 0))
    lam = np.zeros((8, 128, 1024), np.float32)
    for o in range(4):
        r = np.arange(128)[:, None]
        jj = np.arange(1024)[None, :]
        a = o * 128 + r
        gen = np.where(jj < 512, np.where(a > jj, -1e30, 0.0),
                       np.where(a < (jj - 512), -1e30, 0.0)).astype(np.float32)
        lam[4 + o] = gen
        w0 = gen.copy()
        w0[:, :512] = -1e30
        lam[o] = w0
    add("l2f_lam", lam)
    wqkv2 = g("L2_ffn_wqkv")
    add("l2f_wqkv", np.stack([wqkv2[l].reshape(8, 128, 1536) for l in range(2)]))
    add("l2f_qsr", np.stack([np.tile(g("L2_ffn_q_scale")[l], 8) for l in range(2)]))
    add("l2f_ksr", np.stack([np.tile(g("L2_ffn_k_scale")[l], 8) for l in range(2)]))
    add("l2f_wop", np.stack([pack_lhsT_in_out(g("L2_ffn_wo")[l]) for l in range(2)]))
    add("l2f_lg", np.stack([pack_scale(g("L2_ffn_ln_g")[l]) for l in range(2)]))
    add("l2f_lb", np.stack([pack_scale(g("L2_ffn_ln_b")[l]) for l in range(2)]))

    def padc(a, cols):
        out = np.zeros((a.shape[0], cols), np.float32)
        out[:, :a.shape[1]] = a
        return out

    def padr(a, rows):
        out = np.zeros((rows, a.shape[1]), np.float32)
        out[:a.shape[0]] = a
        return out
    w12 = g("L2_ffn_w1")
    w22 = g("L2_ffn_w2")
    b12 = g("L2_ffn_b1")
    add("l2f_w1a", np.stack([pack_lhsT_in_out(padc(w12[l][:, :HID], 2816))
                             for l in range(2)]))
    add("l2f_w1g", np.stack([pack_lhsT_in_out(padc(w12[l][:, HID:], 2816))
                             for l in range(2)]))
    add("l2f_w2p", np.stack([pack_lhsT_in_out(padr(w22[l], 2816))
                             for l in range(2)]))
    pad22 = lambda a: pack_scale(np.concatenate(
        [a, np.zeros(2816 - HID, np.float32)]))
    add("l2f_b1a", np.stack([pad22(b12[l][:HID]) for l in range(2)]))
    add("l2f_b1g", np.stack([pad22(b12[l][HID:]) for l in range(2)]))
    add("l2f_b2", np.stack([pack_scale(g("L2_ffn_b2")[l]) for l in range(2)]))

    add("l3a_wqp", pack_lhsT_out_in(g("L3_attn_Wq")))
    add("l3a_wkp", pack_lhsT_out_in(g("L3_attn_Wk")))
    add("l3a_wvp", pack_lhsT_out_in(g("L3_attn_Wv")))
    add("l3a_wop", pack_lhsT_out_in(g("L3_attn_Wo")))
    add("l3a_xsa", np.broadcast_to(
        np.asarray(inputs["L3_attn_xsa_scale"], np.float32).reshape(1, 1),
        (2, 1)).copy())

    E3, F3 = 8, 512
    add("l3f_wrp", g("L3_ffn_Wr").reshape(8, 128, 8))
    add("l3f_ebr", np.asarray(inputs["L3_ffn_expert_bias"],
                              np.float32).reshape(1, 8))
    Wg3 = g("L3_ffn_Wg")
    Wu3 = g("L3_ffn_Wu")
    add("l3f_wgp", np.concatenate([pack_lhsT_in_out(Wg3[e]) for e in range(E3)], 0))
    add("l3f_wup", np.concatenate([pack_lhsT_in_out(Wu3[e]) for e in range(E3)], 0))
    add("l3f_sgp", pack_lhsT_in_out(g("L3_ffn_Sg")))
    add("l3f_sup", pack_lhsT_in_out(g("L3_ffn_Su")))
    add("l3f_wdall", pack_lhsT_in_out(np.concatenate(
        [g("L3_ffn_Wd").reshape(E3 * F3, D), g("L3_ffn_Sd")], 0)))

    specs = {k: (v.shape, I32 if v.dtype == np.int32 else F32)
             for k, v in packs.items()}
    return packs, specs


EMITTERS = {
    "l0f": lambda cx: emit_l0ffn(cx, "l0f_", 0.5),
    "l0a": lambda cx: emit_l0attn(cx, "l0a_"),
    "l1f": lambda cx: emit_l1ffn(cx, "l1f_", 0.5),
    "l1a": lambda cx: emit_l1attn(cx, "l1a_"),
    "l2f": lambda cx: emit_l2ffn(cx, "l2f_", 0),
    "l2a": lambda cx: emit_l2attn(cx, "l2a_"),
    "l3f": lambda cx: emit_l3ffn(cx, "l3f_", 0.5),
    "l3a": lambda cx: emit_l3attn(cx, "l3a_"),
}

SUBSEQ = ["l0f", "l0a", "l0f", "l1f", "l1a", "l1f",
          "l2f", "l2a", "l2f", "l3f", "l3a", "l3f"]

_BUILD_CACHE = {}


def _sub_inputs(packs, key):
    sub = {n: packs[n] for n in packs if n.startswith(key + "_")}
    sub["ident"] = packs["ident"]
    return sub


def kernel(**inputs):
    T = TPROD
    packs, specs = _prep_all(inputs)
    x = np.asarray(inputs["L0_attn_x"], np.float32)
    xTs = [np.ascontiguousarray(x[i].T) for i in range(8)]
    from concourse.bass_utils import run_bass_kernel_spmd
    if FUSED:
        if "fused" not in _BUILD_CACHE:
            def emit_all(cx):
                for i, key in enumerate(SUBSEQ):
                    EMITTERS[key](cx)
                    cx.fl.fence()
            _BUILD_CACHE["fused"] = build(T, emit_all, specs)
        nc = _BUILD_CACHE["fused"]
        in_maps = [dict(packs, xT=xTs[i]) for i in range(8)]
        r = run_bass_kernel_spmd(nc, in_maps, core_ids=list(range(8)))
        xTs = [np.asarray(r.results[i]["yout"]) for i in range(8)]
    else:
        for key in SUBSEQ:
            if key not in _BUILD_CACHE:
                sspec = {n: specs[n] for n in _sub_inputs(packs, key)}
                _BUILD_CACHE[key] = build(T, EMITTERS[key], sspec)
            nc = _BUILD_CACHE[key]
            sub = _sub_inputs(packs, key)
            in_maps = [dict(sub, xT=xTs[i]) for i in range(8)]
            r = run_bass_kernel_spmd(nc, in_maps, core_ids=list(range(8)))
            xTs = [np.asarray(r.results[i]["yout"]) for i in range(8)]
    return np.stack([xTs[i].T for i in range(8)]).astype(np.float32)
